# version 02
# baseline (speedup 1.0000x reference)
import math
from contextlib import ExitStack

import numpy as np
import concourse.bass as bass
import concourse.mybir as mybir
from concourse.bass_utils import run_bass_kernel_spmd

F32 = mybir.dt.float32
BF16 = mybir.dt.bfloat16
I32 = mybir.dt.int32
AF = mybir.ActivationFunctionType
ALU = mybir.AluOpType
AX = mybir.AxisListType

S = 2048
Dm = 2048
IFF = 4096
NE = 8
NH = 16
NKV = 4
HD = 128
NQ = 4
TQ = 512
IG = 4
NIC = IFF // 128
NDC = Dm // 128
LNEPS = 1e-5
RMSEPS = 1.1920929e-07
TOPP = 0.9
CSC = 1.0 / math.sqrt(HD)
NEG = -1e30
TWO_PI_HI = float(np.float32(2 * math.pi))
TWO_PI_LO = float(np.float64(2 * math.pi) - np.float64(np.float32(2 * math.pi)))


def rope_host_tables():
    t = (np.arange(16)[None, :, None] * 128 + np.arange(128)[:, None, None])
    j = np.arange(64)[None, None, :]
    inv = 1.0 / 10000.0 ** (j / 64.0)
    fr = (t * inv).astype(np.float64)
    ks = np.round(fr / (2 * np.pi))
    kc = np.round((fr + np.pi / 2) / (2 * np.pi))
    return (ks.reshape(128, 1024).astype(np.float32),
            kc.reshape(128, 1024).astype(np.float32))


def r3(ap, blk):
    return ap.rearrange("p (a b) -> p a b", b=blk)


def build_nc(stop="full", dbg=()):
    nc = bass.Bass("TRN2", target_bir_lowering=False, debug=False)
    ctx = ExitStack()

    x_in = nc.dram_tensor("x", [S, Dm], F32, kind="ExternalInput").ap()
    wq_in = nc.dram_tensor("wq", [Dm, Dm], F32, kind="ExternalInput").ap()
    bq_in = nc.dram_tensor("bq", [Dm], F32, kind="ExternalInput").ap()
    wk_in = nc.dram_tensor("wk", [512, Dm], F32, kind="ExternalInput").ap()
    bk_in = nc.dram_tensor("bk", [512], F32, kind="ExternalInput").ap()
    wv_in = nc.dram_tensor("wv", [512, Dm], F32, kind="ExternalInput").ap()
    bv_in = nc.dram_tensor("bv", [512], F32, kind="ExternalInput").ap()
    wo_in = nc.dram_tensor("wo", [Dm, Dm], F32, kind="ExternalInput").ap()
    bo_in = nc.dram_tensor("bo", [Dm], F32, kind="ExternalInput").ap()
    qg_in = nc.dram_tensor("qg", [NH], F32, kind="ExternalInput").ap()
    rw_in = nc.dram_tensor("rw", [NE, Dm], F32, kind="ExternalInput").ap()
    gw_in = nc.dram_tensor("gw", [NE, IFF, Dm], F32, kind="ExternalInput").ap()
    uw_in = nc.dram_tensor("uw", [NE, IFF, Dm], F32, kind="ExternalInput").ap()
    dw_in = nc.dram_tensor("dw", [NE, Dm, IFF], F32, kind="ExternalInput").ap()
    y_out = nc.dram_tensor("y", [S, Dm], F32, kind="ExternalOutput").ap()

    def scrt(name, shape):
        kind = "ExternalOutput" if name in dbg else "Internal"
        return nc.dram_tensor(name, shape, F32, kind=kind).ap()

    def scrtb(name, shape):
        kind = "ExternalOutput" if name in dbg else "Internal"
        return nc.dram_tensor(name, shape, BF16, kind=kind).ap()
    gTh = scrtb("gTh", [NE, NIC, 128, Dm])
    gTl = scrtb("gTl", [NE, NIC, 128, Dm])
    uTh = scrtb("uTh", [NE, NIC, 128, Dm])
    uTl = scrtb("uTl", [NE, NIC, 128, Dm])
    dTh = scrtb("dTh", [NE, NIC, 128, Dm])
    dTl = scrtb("dTl", [NE, NIC, 128, Dm])
    qTw = scrt("qTw", [16, 128, Dm])
    kTw = scrt("kTw", [4, 128, Dm])
    vTw = scrt("vTw", [4, 128, Dm])
    oTw = scrt("oTw", [16, 128, Dm])
    rTw = scrt("rTw", [128, 128])
    xlnfd = scrt("xlnfd", [128, 32768])
    x1s = scrt("x1s", [S, Dm])
    x2s = scrt("x2s", [S, Dm])
    qkvs = scrt("qkvs", [24, 128, S])
    aos = scrt("aos", [16, 128, S])

    ks_np, kc_np = rope_host_tables()
    ksd = nc.inline_tensor(ks_np, name="ks_d").ap()
    kcd = nc.inline_tensor(kc_np, name="kc_d").ap()
    idn = nc.inline_tensor(np.eye(128, dtype=np.float32), name="idn_d").ap()

    ident = nc.alloc_sbuf_tensor("ident", [128, 128], F32).ap()
    ones1 = nc.alloc_sbuf_tensor("ones1", [1, 128], F32).ap()
    qgr = nc.alloc_sbuf_tensor("qgr", [128, 16], F32).ap()
    biasq = nc.alloc_sbuf_tensor("biasq", [128, 16], F32).ap()
    biask = nc.alloc_sbuf_tensor("biask", [128, 4], F32).ap()
    biasv = nc.alloc_sbuf_tensor("biasv", [128, 4], F32).ap()
    stats = nc.alloc_sbuf_tensor("stats", [128, 32], F32).ap()
    small = nc.alloc_sbuf_tensor("small", [128, 128], F32).ap()
    ioti = nc.alloc_sbuf_tensor("ioti", [128, 512], I32).ap()
    cosT = nc.alloc_sbuf_tensor("cosT", [128, 1024], F32).ap()
    sinT = nc.alloc_sbuf_tensor("sinT", [128, 1024], F32).ap()
    rt_sb = nc.alloc_sbuf_tensor("rt_sb", [128, 128], F32).ap()
    rthl = nc.alloc_sbuf_tensor("rthl", [128, 256], BF16).ap()
    BIGC = 45184
    BIGt = nc.alloc_sbuf_tensor("BIG", [128, BIGC], F32)
    BIG = BIGt.ap()
    big_addr = nc.lookup_mloc(BIGt).addr
    BFOFF = 16512
    BFC = (BIGC - BFOFF) * 2
    PSg = ctx.enter_context(nc.psum_tensor("PSb", [128, 4096], F32))
    PS = PSg

    def bigv(off, n):
        return BIG[:, off:off + n]



    def mklay(*items):
        d, off = {}, 0
        for name, n in items:
            d[name] = off
            off += n
        assert off <= BIGC, (off, items[0])
        return d

    L = mklay(("ACC", 8192), ("WREPA", 4096), ("LNX0", 2048),
              ("GSB0", 512), ("GSB1", 512), ("HTMP", 512),
              ("WROW", 512), ("WALL", 32))
    assert L["WALL"] + 32 <= BFOFF
    FBT = {}
    off2 = 0
    for name, n in [("XLH", 8192), ("XLL", 8192),
                    ("WGH0", 2048), ("WGH1", 2048), ("WGL0", 2048), ("WGL1", 2048),
                    ("WUH0", 2048), ("WUH1", 2048), ("WUL0", 2048), ("WUL1", 2048),
                    *[(f"WDH{k}", 2048) for k in range(IG)],
                    *[(f"WDL{k}", 2048) for k in range(IG)],
                    ("HBH", 2048), ("HBL", 2048)]:
        FBT[name] = nc.alloc_sbuf_tensor_at(
            "bf_" + name, [128, n], BF16,
            offset=big_addr + BFOFF * 4 + off2 * 2).ap()
        off2 += n
    assert off2 <= BFC, (off2, BFC)
    P0_BF_MIN = (32768 - BFOFF) * 2
    for k, coff in enumerate([32768, 34816, 36864, 38912]):
        assert coff >= P0_BF_MIN
        nm = ("CONVH0", "CONVL0", "CONVH1", "CONVL1")[k]
        FBT[nm] = nc.alloc_sbuf_tensor_at(
            "bf_" + nm, [128, 2048], BF16,
            offset=big_addr + BFOFF * 4 + coff * 2).ap()
    A = mklay(("XLNF", 32768), ("WPJ0", 2048), ("WPJ1", 2048),
              ("QR", 2048), ("TOK", 2048), ("TMPA", 2048), ("TMPB", 2048))
    C = mklay(("KT", 2048), ("VT", 2048), ("VG", 2048), ("QT", 2048),
              ("SS", 2048), ("PP", 2048), ("PT", 8192), ("AOT", 2048),
              ("MASKS", 2048))
    O = mklay(("AOTF", 32768), ("WOT0", 2048), ("WOT1", 2048),
              ("BOREP", 2048), ("XR0", 2048), ("XR1", 2048))
    P0 = mklay(("NAT0", 8192), ("NAT1", 8192),
               *[(f"ASM{k}", 2048) for k in range(8)])

    class Bld:
        def __init__(self):
            self.sm, self.c = {}, {}
            for nm in ["LD", "LG", "TE", "VE", "SE", "GE", "BR",
                       "DF", "SIL", "HH", "FL", "MG", "MU", "MD"]:
                self.sm[nm] = ctx.enter_context(nc.semaphore("s" + nm))
                self.c[nm] = 0

        def inc(self, ins, s, n=1):
            ins.then_inc(self.sm[s], n)
            self.c[s] += n
            return self.c[s]

        def dma(self, eng, out, in_, s):
            return self.inc(eng.dma_start(out=out, in_=in_), s, 16)

        def wait(self, eng, s, v=None):
            v = self.c[s] if v is None else v
            if v > 0:
                eng.wait_ge(self.sm[s], v)

        def barrier(self):
            engs = [nc.sync, nc.tensor, nc.vector, nc.scalar, nc.gpsimd]
            for e in engs:
                self.inc(e.nop(), "BR", 1)
            v = self.c["BR"]
            for e in engs:
                e.wait_ge(self.sm["BR"], v)

    b = Bld()
    sy, te, ve, se, gp = nc.sync, nc.tensor, nc.vector, nc.scalar, nc.gpsimd

    b.dma(sy, ident[:], idn[:], "LD")
    b.wait(ve, "LD")
    ins = ve.memset(ones1[:], 1.0)
    b.inc(ins, "VE")

    nat_rel = []
    slab_i = [0]
    asm_free = {}
    conv_free = {}
    conv_cnt = [0]

    def do_slab(dmf, nsub, ninner, col_fn, outs, tile_cols, nat_part=128,
                bf_outs=None):
        si = slab_i[0]
        slab_i[0] += 1
        natv = bigv(P0[f"NAT{si % 2}"], 8192)
        if si >= 2:
            b.wait(sy, "TE", nat_rel[si - 2])
        dmf(natv)
        b.wait(te, "LD")
        for j in range(nsub):
            slot = (si % 2) * 4 + j
            av = bigv(P0[f"ASM{slot}"], 2048)
            if slot in asm_free:
                sm_, vv_ = asm_free[slot]
                b.wait(ve, sm_, vv_)
            for i0 in range(0, ninner, 4):
                bank = (i0 // 4) % 2
                pv = PS[:, bank * 512:(bank + 1) * 512]
                b.wait(te, "VE")
                last = None
                for q in range(4):
                    i = i0 + q
                    if i >= ninner:
                        break
                    src = natv[0:nat_part, col_fn(j, i)]
                    w = src.shape[-1]
                    last = te.transpose(pv[0:w, q * nat_part:q * nat_part + nat_part],
                                        src, ident[:])
                b.inc(last, "TE")
                b.wait(ve, "TE")
                n = min(4, ninner - i0) * nat_part
                ins = ve.tensor_copy(av[:, i0 * nat_part:i0 * nat_part + n], pv[:, 0:n])
                b.inc(ins, "VE")
            if bf_outs is None:
                b.wait(se, "VE")
                b.dma(se, outs[j], av[:, 0:tile_cols], "LG")
                asm_free[slot] = ("LG", b.c["LG"])
            else:
                p = conv_cnt[0] % 2
                conv_cnt[0] += 1
                ch = FBT[f"CONVH{p}"]
                cl = FBT[f"CONVL{p}"]
                if p in conv_free:
                    b.wait(se, "LG", conv_free[p])
                    b.wait(ve, "LG", conv_free[p])
                b.wait(se, "VE")
                ins = se.copy(ch, av[:, 0:2048])
                b.inc(ins, "SE")
                b.wait(ve, "SE")
                ins = ve.tensor_tensor(cl, av[:, 0:2048], ch, ALU.subtract)
                b.inc(ins, "VE")
                b.wait(ve, "VE")
                b.inc(ve.nop(), "VE")
                b.wait(se, "VE")
                hi_t, lo_t = bf_outs[j]
                b.dma(se, hi_t, ch, "LG")
                b.dma(se, lo_t, cl, "LG")
                conv_free[p] = b.c["LG"]
                asm_free[slot] = ("VE", b.c["VE"])
        nat_rel.append(b.c["TE"])

    def image_weight(w_nat, O_rows, tiles_out, bf_tiles=None):
        for s_i in range(O_rows // 512):
            def dmf(natv, s_i=s_i):
                for j in range(4):
                    b.dma(sy, natv[:, j * 2048:(j + 1) * 2048],
                          w_nat[s_i * 512 + j * 128: s_i * 512 + (j + 1) * 128, :],
                          "LD")
            do_slab(dmf, 4, 16,
                    lambda j, i: slice(j * 2048 + i * 128, j * 2048 + (i + 1) * 128),
                    [tiles_out[s_i * 4 + j] for j in range(4)] if tiles_out else None,
                    2048,
                    bf_outs=None if bf_tiles is None else
                    [bf_tiles[s_i * 4 + j] for j in range(4)])

    def plain_weight(w_nat, I_cols, tiles_out, bf_tiles=None):
        for s_i in range(I_cols // 512):
            def dmf(natv, s_i=s_i):
                src = w_nat[:, s_i * 512:(s_i + 1) * 512].rearrange(
                    "(a p) c -> p a c", p=128)
                b.dma(sy, r3(natv, 512), src, "LD")
            do_slab(dmf, 4, 16,
                    lambda j, i: slice(i * 512 + j * 128, i * 512 + j * 128 + 128),
                    [tiles_out[s_i * 4 + j] for j in range(4)] if tiles_out else None,
                    2048,
                    bf_outs=None if bf_tiles is None else
                    [bf_tiles[s_i * 4 + j] for j in range(4)])

    for e in range(NE):
        image_weight(gw_in[e], IFF, None,
                     bf_tiles=[(gTh[e, ic], gTl[e, ic]) for ic in range(NIC)])
        image_weight(uw_in[e], IFF, None,
                     bf_tiles=[(uTh[e, ic], uTl[e, ic]) for ic in range(NIC)])
        plain_weight(dw_in[e], IFF, None,
                     bf_tiles=[(dTh[e, ic], dTl[e, ic]) for ic in range(NIC)])
    image_weight(wq_in, Dm, [qTw[i] for i in range(16)])
    image_weight(wk_in, 512, [kTw[i] for i in range(4)])
    image_weight(wv_in, 512, [vTw[i] for i in range(4)])
    plain_weight(wo_in, Dm, [oTw[i] for i in range(16)])
    si = slab_i[0]
    slab_i[0] += 1
    natv = bigv(P0[f"NAT{si % 2}"], 8192)
    if si >= 2:
        b.wait(sy, "TE", nat_rel[si - 2])
    b.dma(sy, natv[0:8, 0:2048], rw_in[:, :], "LD")
    b.wait(te, "LD")
    av = bigv(P0[f"ASM{(si % 2) * 4}"], 2048)
    for i0 in range(0, 16, 4):
        bank = (i0 // 4) % 2
        pv = PS[:, bank * 512:(bank + 1) * 512]
        b.wait(te, "VE")
        for q in range(4):
            dc = i0 + q
            last = te.transpose(pv[:, q * 8:(q + 1) * 8],
                                natv[0:8, dc * 128:(dc + 1) * 128], ident[0:8, 0:8])
        b.inc(last, "TE")
        b.wait(ve, "TE")
        ins = ve.tensor_copy(av[:, i0 * 8:i0 * 8 + 32], pv[:, 0:32])
        b.inc(ins, "VE")
    b.wait(se, "VE")
    b.dma(se, rTw[:, :], av[:, 0:128], "LG")
    nat_rel.append(b.c["TE"])

    b.barrier()
    if stop == "p0":
        b.wait(se, "LG")
        ctx.close()
        return nc

    lnx_rel = {}
    acc_store = [0]
    gate_rel = {}
    up_rel = {}
    wbuf_rel = {}
    dgrp = [0]
    grp_fl = {}
    fl_rel = {}
    gi_ctr = [0]

    b.dma(sy, rt_sb[:], rTw[:, :], "LD")
    rt_ld = b.c["LD"]
    b.wait(ve, "LD", rt_ld)
    ins = ve.tensor_copy(rthl[:, 0:128], rt_sb[:])
    b.inc(ins, "VE")
    b.wait(ve, "VE")
    ins = ve.tensor_tensor(rthl[:, 128:256], rt_sb[:], rthl[:, 0:128], ALU.subtract)
    b.inc(ins, "VE")
    b.wait(ve, "VE")
    rt_split = b.c["VE"]

    def emit_ln_quarter(src, q, dest_off, dest_blk, split=False, lx_off=None):
        for tc in range(4):
            t0 = q * TQ + tc * 128
            lx = bigv(L["LNX0"] if lx_off is None else lx_off, 2048)
            key = 0
            if key in lnx_rel:
                b.wait(se, "TE", lnx_rel[key])
            b.wait(se, "VE")
            b.dma(se, lx, src[t0:t0 + 128, :], "LG")
            b.wait(ve, "LG")
            for i in range(4):
                ins = ve.bn_stats(stats[:, i * 6:(i + 1) * 6], lx[:, i * 512:(i + 1) * 512])
            b.inc(ins, "VE")
            b.wait(ve, "VE")
            ins = ve.bn_aggr(stats[:, 24:26], stats[:, 0:24])
            b.inc(ins, "VE")
            b.wait(ve, "VE")
            ins = ve.tensor_scalar(small[:, 0:1], stats[:, 25:26], LNEPS, None, ALU.add)
            b.inc(ins, "VE")
            b.wait(se, "VE")
            ins = se.activation(small[:, 1:2], small[:, 0:1], AF.Sqrt)
            b.inc(ins, "SE")
            b.wait(ve, "SE")
            ins = ve.reciprocal(small[:, 2:3], small[:, 1:2])
            b.inc(ins, "VE")
            b.wait(ve, "VE")
            ins = ve.tensor_scalar(lx, lx, stats[:, 24:25], small[:, 2:3],
                                   ALU.subtract, ALU.mult)
            b.inc(ins, "VE")
            b.wait(te, "VE")
            for i0 in range(0, 16, 4):
                bank = 2 + (i0 // 4) % 2
                pv = PS[:, bank * 512:(bank + 1) * 512]
                b.wait(te, "VE")
                for qq in range(4):
                    dc = i0 + qq
                    last = te.transpose(pv[:, qq * 128:(qq + 1) * 128],
                                        lx[:, dc * 128:(dc + 1) * 128], ident[:])
                b.inc(last, "TE")
                b.wait(ve, "TE")
                if not split:
                    toff = (q * TQ if dest_blk == 2048 else 0) + tc * 128
                    dst = r3(BIG[:, dest_off + i0 * dest_blk:
                                 dest_off + (i0 + 4) * dest_blk], dest_blk)[
                        :, :, toff: toff + 128]
                    ins = ve.tensor_copy(dst, r3(pv, 128))
                    b.inc(ins, "VE")
                else:
                    toff = tc * 128
                    dsth = r3(FBT["XLH"][:, i0 * 512:(i0 + 4) * 512], 512)[
                        :, :, toff: toff + 128]
                    dstl = r3(FBT["XLL"][:, i0 * 512:(i0 + 4) * 512], 512)[
                        :, :, toff: toff + 128]
                    ins = ve.tensor_copy(dsth, r3(pv, 128))
                    b.inc(ins, "VE")
                    b.wait(ve, "VE")
                    ins = ve.tensor_tensor(dstl, r3(pv, 128), dsth, ALU.subtract)
                    b.inc(ins, "VE")
                    b.wait(ve, "VE")
            lnx_rel[key] = b.c["TE"]

    def emit_router(q):
        wall = bigv(L["WALL"], 32)
        for tc in range(4):
            pv = PS[:, 2048:2056]
            b.wait(te, "VE")
            b.wait(te, "VE", rt_split)
            b.wait(te, "FL")
            for dc in range(NDC):
                xh = FBT["XLH"][:, dc * 512 + tc * 128: dc * 512 + tc * 128 + 128]
                xl = FBT["XLL"][:, dc * 512 + tc * 128: dc * 512 + tc * 128 + 128]
                rh = rthl[:, dc * 8:dc * 8 + 8]
                rl = rthl[:, 128 + dc * 8:128 + dc * 8 + 8]
                te.matmul(pv, xh, rh, start=(dc == 0), stop=False)
                te.matmul(pv, xl, rh, start=False, stop=False)
                last = te.matmul(pv, xh, rl, start=False,
                                 stop=(dc == NDC - 1))
            b.inc(last, "TE")
            b.wait(ve, "TE")
            pr = small[:, 8:16]
            ins = ve.tensor_copy(small[:, 16:24], pv[:, 0:8])
            b.inc(ins, "VE")
            b.wait(ve, "VE")
            ins = ve.reduce_max(small[:, 0:1], small[:, 16:24], AX.X)
            b.inc(ins, "VE")
            b.wait(ve, "VE")
            ins = ve.tensor_scalar(small[:, 1:2], small[:, 0:1], -1.0, None, ALU.mult)
            b.inc(ins, "VE")
            b.wait(se, "VE")
            ins = se.activation(pr, small[:, 16:24], AF.Exp, bias=small[:, 1:2],
                                scale=1.0)
            b.inc(ins, "SE")
            b.wait(ve, "SE")
            ins = ve.reduce_sum(small[:, 2:3], pr, AX.X)
            b.inc(ins, "VE")
            b.wait(ve, "VE")
            ins = ve.reciprocal(small[:, 3:4], small[:, 2:3])
            b.inc(ins, "VE")
            b.wait(ve, "VE")
            ins = ve.tensor_scalar(pr, pr, small[:, 3:4], None, ALU.mult)
            b.inc(ins, "VE")
            b.wait(ve, "VE")
            for e in range(NE):
                ins = ve.scalar_tensor_tensor(small[:, 24:32], pr, pr[:, e:e + 1], pr,
                                              ALU.is_gt, ALU.mult)
                b.inc(ins, "VE")
                b.wait(ve, "VE")
                ins = ve.reduce_sum(small[:, 4:5], small[:, 24:32], AX.X)
                b.inc(ins, "VE")
                b.wait(ve, "VE")
                ins = ve.tensor_scalar(small[:, 5:6], small[:, 4:5], TOPP, None, ALU.is_le)
                b.inc(ins, "VE")
                b.wait(ve, "VE")
                ins = ve.tensor_tensor(wall[:, tc * 8 + e: tc * 8 + e + 1],
                                       small[:, 5:6], pr[:, e:e + 1], ALU.mult)
            b.inc(ins, "VE")

    def emit_wrep_all(q):
        wall = bigv(L["WALL"], 32)
        wrow = bigv(L["WROW"], 512)
        for e in range(NE):
            pv = PS[:, 3584:4096]
            b.wait(te, "VE")
            b.wait(te, "FL")
            for tc in range(4):
                last = te.transpose(pv[0:1, tc * 128:(tc + 1) * 128],
                                    wall[:, tc * 8 + e: tc * 8 + e + 1], ident[:])
            b.inc(last, "TE")
            b.wait(ve, "TE")
            ins = ve.tensor_copy(wrow[0:1, :], pv[0:1, :])
            b.inc(ins, "VE")
            b.wait(te, "VE")
            ins = te.matmul(pv, ones1[:], wrow[0:1, :], start=True, stop=True)
            b.inc(ins, "TE")
            b.wait(ve, "TE")
            ins = ve.tensor_copy(bigv(L["WREPA"] + e * 512, 512), pv)
            b.inc(ins, "VE")
        b.wait(te, "VE")

    def emit_ffn(call, src, dst):
        for q in range(NQ):
            b.wait(ve, "MU")
            b.wait(ve, "TE")
            emit_ln_quarter(src, q, 0, 512, split=True)
            emit_router(q)
            emit_wrep_all(q)
            for e in range(NE):
                for ig in range(NIC // IG):
                    g = dgrp[0]
                    for il in range(IG):
                        i = ig * IG + il
                        gi = gi_ctr[0]
                        gi_ctr[0] += 1
                        p = gi % 2
                        if gi >= 2:
                            b.wait(sy, "MU", wbuf_rel.get(p, 0))
                        wgh = FBT[f"WGH{p}"]
                        wgl = FBT[f"WGL{p}"]
                        wuh = FBT[f"WUH{p}"]
                        wul = FBT[f"WUL{p}"]
                        b.dma(sy, wgh, gTh[e, i], "LD")
                        b.dma(sy, wgl, gTl[e, i], "LD")
                        b.dma(sy, wuh, uTh[e, i], "LD")
                        b.dma(sy, wul, uTl[e, i], "LD")
                        wgu_done = b.c["LD"]
                        if g >= 1:
                            b.wait(sy, "FL", grp_fl[g - 1])
                        b.dma(sy, FBT[f"WDH{il}"], dTh[e, i], "LD")
                        b.dma(sy, FBT[f"WDL{il}"], dTl[e, i], "LD")
                        pg = PS[:, p * 512:(p + 1) * 512]
                        pu = PS[:, 1024 + p * 512:1024 + (p + 1) * 512]
                        b.wait(te, "LD", wgu_done)
                        if gi >= 2:
                            b.wait(te, "SIL", gate_rel.get(p, 0))
                        for dc in range(NDC):
                            xh = FBT["XLH"][:, dc * 512:(dc + 1) * 512]
                            xl = FBT["XLL"][:, dc * 512:(dc + 1) * 512]
                            wh = wgh[:, dc * 128:(dc + 1) * 128]
                            wl = wgl[:, dc * 128:(dc + 1) * 128]
                            te.matmul(pg, wh, xh, start=(dc == 0), stop=False)
                            te.matmul(pg, wh, xl, start=False, stop=False)
                            last = te.matmul(pg, wl, xh, start=False,
                                             stop=(dc == NDC - 1))
                        b.inc(last, "MG")
                        if gi >= 2:
                            b.wait(te, "HH", up_rel.get(p, 0))
                        for dc in range(NDC):
                            xh = FBT["XLH"][:, dc * 512:(dc + 1) * 512]
                            xl = FBT["XLL"][:, dc * 512:(dc + 1) * 512]
                            wh = wuh[:, dc * 128:(dc + 1) * 128]
                            wl = wul[:, dc * 128:(dc + 1) * 128]
                            te.matmul(pu, wh, xh, start=(dc == 0), stop=False)
                            te.matmul(pu, wh, xl, start=False, stop=False)
                            last = te.matmul(pu, wl, xh, start=False,
                                             stop=(dc == NDC - 1))
                        b.inc(last, "MU")
                        wbuf_rel[p] = b.c["MU"]
                        gsb = bigv(L[f"GSB{p}"], 512)
                        b.wait(se, "MG")
                        if gi >= 2:
                            b.wait(se, "HH", up_rel.get(p, 0))
                        ins = se.activation(gsb, pg, AF.Silu)
                        b.inc(ins, "SIL")
                        gate_rel[p] = b.c["SIL"]
                        htmp = bigv(L["HTMP"], 512)
                        hbh = FBT["HBH"][:, il * 512:(il + 1) * 512]
                        hbl = FBT["HBL"][:, il * 512:(il + 1) * 512]
                        b.wait(ve, "SIL")
                        b.wait(ve, "MU")
                        if g >= 1:
                            b.wait(ve, "FL", grp_fl[g - 1])
                        ve.tensor_tensor(htmp, gsb, pu, ALU.mult)
                        ve.tensor_tensor(htmp, htmp, bigv(L["WREPA"] + e * 512, 512),
                                         ALU.mult)
                        ins = ve.tensor_copy(hbh, htmp)
                        b.inc(ins, "VE")
                        b.wait(ve, "VE")
                        ins = ve.tensor_tensor(hbl, htmp, hbh, ALU.subtract)
                        b.inc(ins, "HH")
                        up_rel[p] = b.c["HH"]
                    b.wait(te, "HH")
                    b.wait(te, "LD")
                    for tc in range(4):
                        for dg in range(4):
                            key = dg
                            if key in fl_rel:
                                b.wait(te, "FL", fl_rel[key])
                            pd = PS[:, 2048 + dg * 512:2048 + (dg + 1) * 512]
                            for il in range(IG):
                                hh_ = FBT["HBH"][:, il * 512 + tc * 128:
                                                 il * 512 + tc * 128 + 128]
                                hl_ = FBT["HBL"][:, il * 512 + tc * 128:
                                                 il * 512 + tc * 128 + 128]
                                dh_ = FBT[f"WDH{il}"][:, dg * 512:(dg + 1) * 512]
                                dl_ = FBT[f"WDL{il}"][:, dg * 512:(dg + 1) * 512]
                                te.matmul(pd, hh_, dh_, start=(il == 0), stop=False)
                                te.matmul(pd, hl_, dh_, start=False, stop=False)
                                last = te.matmul(pd, hh_, dl_, start=False,
                                                 stop=(il == IG - 1))
                            b.inc(last, "MD")
                            b.wait(ve, "MD")
                            accs = BIG[:, L["ACC"] + tc * 2048 + dg * 512:
                                       L["ACC"] + tc * 2048 + dg * 512 + 512]
                            if e == 0 and ig == 0:
                                b.wait(ve, "LG", acc_store[0])
                                ins = ve.tensor_copy(accs, pd)
                            else:
                                ins = ve.tensor_tensor(accs, accs, pd, ALU.add)
                            b.inc(ins, "FL")
                            b.wait(ve, "FL")
                            fl_rel[key] = b.c["FL"]
                    grp_fl[g] = b.c["FL"]
                    dgrp[0] += 1
            for tc in range(4):
                t0 = q * TQ + tc * 128
                lx = bigv(L["LNX0"], 2048)
                key = 0
                if key in lnx_rel:
                    b.wait(se, "TE", lnx_rel[key])
                b.wait(se, "VE")
                b.dma(se, lx, src[t0:t0 + 128, :], "LG")
                b.wait(ve, "LG")
                b.wait(ve, "FL")
                accs = BIG[:, L["ACC"] + tc * 2048:L["ACC"] + (tc + 1) * 2048]
                ins = ve.scalar_tensor_tensor(accs, accs, 0.5, lx, ALU.mult, ALU.add)
                b.inc(ins, "VE")
                b.wait(ve, "VE")
                b.inc(ve.nop(), "VE")
                b.wait(se, "VE")
                b.dma(se, dst[t0:t0 + 128, :], accs, "LG")
            acc_store[0] = b.c["LG"]

    emit_ffn(0, x_in, x1s)
    b.barrier()
    if stop == "ffn1":
        b.wait(se, "LG")
        ctx.close()
        return nc

    def build_rope_tables():
        ins = gp.iota(ioti[:, 0:16], [[128, 16]], base=0, channel_multiplier=1)
        b.inc(ins, "GE")
        ins = gp.iota(ioti[0:1, 16:80], [[1, 64]], base=0, channel_multiplier=0)
        b.inc(ins, "GE")
        b.wait(ve, "GE")
        tv = bigv(A["TMPA"], 2048)
        ve.tensor_copy(tv[:, 0:16], ioti[:, 0:16])
        ins = ve.tensor_copy(tv[0:1, 16:80], ioti[0:1, 16:80])
        b.inc(ins, "VE")
        b.wait(se, "VE")
        ins = se.activation(tv[0:1, 80:144], tv[0:1, 16:80], AF.Exp,
                            bias=0.0, scale=-math.log(10000.0) / 64.0)
        b.inc(ins, "SE")
        b.wait(te, "SE")
        pv = PS[:, 0:512]
        ins = te.matmul(pv[:, 0:64], ones1[:], tv[0:1, 80:144], start=True, stop=True)
        b.inc(ins, "TE")
        b.wait(ve, "TE")
        inv_rep = bigv(A["TMPB"], 2048)
        ins = ve.tensor_copy(inv_rep[:, 0:64], pv[:, 0:64])
        b.inc(ins, "VE")
        b.wait(ve, "VE")
        fr = bigv(A["QR"], 2048)
        for tc in range(16):
            ins = ve.tensor_scalar(fr[:, tc * 64:(tc + 1) * 64], inv_rep[:, 0:64],
                                   tv[:, tc:tc + 1], None, ALU.mult)
        b.inc(ins, "VE")
        b.wait(ve, "VE")
        ksv = bigv(A["TOK"], 2048)
        b.dma(sy, ksv[:, 0:1024], ksd[:, :], "LD")
        b.dma(sy, ksv[:, 1024:2048], kcd[:, :], "LD")
        b.wait(ve, "LD")
        sa = bigv(A["TMPA"], 2048)
        ins = ve.scalar_tensor_tensor(sa[:, 0:1024], ksv[:, 0:1024], -TWO_PI_HI,
                                      fr[:, 0:1024], ALU.mult, ALU.add)
        b.inc(ins, "VE")
        b.wait(ve, "VE")
        ins = ve.scalar_tensor_tensor(sa[:, 0:1024], ksv[:, 0:1024], -TWO_PI_LO,
                                      sa[:, 0:1024], ALU.mult, ALU.add)
        b.inc(ins, "VE")
        b.wait(ve, "VE")
        ins = ve.tensor_scalar(sa[:, 0:1024], sa[:, 0:1024], math.pi, -math.pi,
                               ALU.min, ALU.max)
        b.inc(ins, "VE")
        b.wait(se, "VE")
        ins = se.activation(sinT[:], sa[:, 0:1024], AF.Sin)
        b.inc(ins, "SE")
        ins = ve.tensor_scalar(sa[:, 1024:2048], fr[:, 0:1024], math.pi / 2, None, ALU.add)
        b.inc(ins, "VE")
        b.wait(ve, "VE")
        ins = ve.scalar_tensor_tensor(sa[:, 1024:2048], ksv[:, 1024:2048], -TWO_PI_HI,
                                      sa[:, 1024:2048], ALU.mult, ALU.add)
        b.inc(ins, "VE")
        b.wait(ve, "VE")
        ins = ve.scalar_tensor_tensor(sa[:, 1024:2048], ksv[:, 1024:2048], -TWO_PI_LO,
                                      sa[:, 1024:2048], ALU.mult, ALU.add)
        b.inc(ins, "VE")
        b.wait(ve, "VE")
        ins = ve.tensor_scalar(sa[:, 1024:2048], sa[:, 1024:2048], math.pi, -math.pi,
                               ALU.min, ALU.max)
        b.inc(ins, "VE")
        b.wait(se, "VE")
        ins = se.activation(cosT[:], sa[:, 1024:2048], AF.Sin)
        b.inc(ins, "SE")
        b.dma(se, small[0:1, 96:112], qg_in.rearrange("(o a) -> o a", o=1), "LG")
        b.wait(te, "LG")
        b.wait(te, "SE")
        ins = te.matmul(pv[:, 0:16], ones1[:], small[0:1, 96:112],
                        start=True, stop=True)
        b.inc(ins, "TE")
        b.wait(ve, "TE")
        ins = ve.tensor_copy(qgr[:], pv[:, 0:16])
        b.inc(ins, "VE")
        b.wait(ve, "VE")
        with nc.allow_non_contiguous_dma(reason="tiny one-time bias image loads"):
            b.dma(se, biasq[:], bq_in.rearrange("(a p) -> p a", p=128), "LG")
            b.dma(se, biask[:], bk_in.rearrange("(a p) -> p a", p=128), "LG")
            b.dma(se, biasv[:], bv_in.rearrange("(a p) -> p a", p=128), "LG")

    def emit_qkv():
        for q in range(NQ):
            emit_ln_quarter(x1s, q, A["XLNF"], 2048, lx_off=A["QR"])
        if "xlnfd" in dbg:
            b.wait(se, "VE")
            b.dma(se, xlnfd[:, :], BIG[:, 0:32768], "LG")
        chunks = ([("q", ec, qTw[ec], biasq[:, ec:ec + 1], ec) for ec in range(16)]
                  + [("k", ec, kTw[ec], biask[:, ec:ec + 1], 16 + ec)
                     for ec in range(4)]
                  + [("v", ec, vTw[ec], biasv[:, ec:ec + 1], 20 + ec)
                     for ec in range(4)])
        wpj_rel = {}
        for j, (kind, ec, wtile, bias_col, srow) in enumerate(chunks):
            p = j % 2
            if j >= 2:
                b.wait(sy, "TE", wpj_rel[p])
            wpj = bigv(A[f"WPJ{p}"], 2048)
            b.dma(sy, wpj, wtile, "LD")
            b.wait(te, "LD")
            qr = bigv(A["QR"], 2048)
            for tg in range(4):
                pv = PS[:, tg * 512:(tg + 1) * 512]
                b.wait(te, "VE")
                for dc in range(NDC):
                    last = te.matmul(pv, wpj[:, dc * 128:(dc + 1) * 128],
                                     bigv(A["XLNF"] + dc * 2048 + tg * 512, 512),
                                     start=(dc == 0), stop=(dc == NDC - 1))
                b.inc(last, "TE")
                b.wait(ve, "TE")
                ins = ve.tensor_scalar(qr[:, tg * 512:(tg + 1) * 512], pv,
                                       bias_col, None, ALU.add)
                b.inc(ins, "VE")
            wpj_rel[p] = b.c["TE"]
            if kind == "v":
                b.wait(se, "VE")
                b.dma(se, qkvs[srow], qr, "LG")
                continue
            tok = bigv(A["TOK"], 2048)
            for i0 in range(0, 16, 4):
                bank = 4 + (i0 // 4) % 2
                pv = PS[:, bank * 512:(bank + 1) * 512]
                b.wait(te, "VE")
                for qq in range(4):
                    tc = i0 + qq
                    last = te.transpose(pv[:, qq * 128:(qq + 1) * 128],
                                        qr[:, tc * 128:(tc + 1) * 128], ident[:])
                b.inc(last, "TE")
                b.wait(ve, "TE")
                ins = ve.tensor_copy(tok[:, i0 * 128:i0 * 128 + 512], pv)
                b.inc(ins, "VE")
            tmpa = bigv(A["TMPA"], 2048)
            tmpb = bigv(A["TMPB"], 2048)
            b.wait(se, "VE")
            ins = se.activation(tmpa, tok, AF.Square)
            b.inc(ins, "SE")
            b.wait(ve, "SE")
            ins = ve.tensor_reduce(small[:, 32:48], r3(tmpa, 128), AX.X, ALU.add)
            b.inc(ins, "VE")
            b.wait(ve, "VE")
            ins = ve.tensor_scalar(small[:, 32:48], small[:, 32:48],
                                   1.0 / HD, RMSEPS, ALU.mult, ALU.add)
            b.inc(ins, "VE")
            b.wait(se, "VE")
            ins = se.activation(small[:, 48:64], small[:, 32:48], AF.Sqrt)
            b.inc(ins, "SE")
            b.wait(ve, "SE")
            ins = ve.reciprocal(small[:, 64:80], small[:, 48:64])
            b.inc(ins, "VE")
            b.wait(ve, "VE")
            for tc in range(16):
                if kind == "q":
                    ins = ve.tensor_scalar(tok[:, tc * 128:(tc + 1) * 128],
                                           tok[:, tc * 128:(tc + 1) * 128],
                                           small[:, 64 + tc:65 + tc],
                                           qgr[:, ec:ec + 1], ALU.mult, ALU.mult)
                else:
                    ins = ve.tensor_scalar(tok[:, tc * 128:(tc + 1) * 128],
                                           tok[:, tc * 128:(tc + 1) * 128],
                                           small[:, 64 + tc:65 + tc], None, ALU.mult)
            b.inc(ins, "VE")
            b.wait(ve, "VE")
            top = r3(tok, 128)[:, :, 0:64]
            bot = r3(tok, 128)[:, :, 64:128]
            cosv = r3(cosT[:], 64)
            sinv = r3(sinT[:], 64)
            t1 = r3(tmpa, 64)[:, 0:16, :]
            t2 = r3(tmpa, 64)[:, 16:32, :]
            t3 = r3(tmpb, 64)[:, 0:16, :]
            t4 = r3(tmpb, 64)[:, 16:32, :]
            ve.tensor_tensor(t1, top, cosv, ALU.mult)
            ve.tensor_tensor(t2, bot, sinv, ALU.mult)
            ve.tensor_tensor(t3, top, sinv, ALU.mult)
            ve.tensor_tensor(t4, bot, cosv, ALU.mult)
            ve.tensor_tensor(top, t1, t2, ALU.add)
            ins = ve.tensor_tensor(bot, t4, t3, ALU.subtract)
            b.inc(ins, "VE")
            b.wait(te, "VE")
            for i0 in range(0, 16, 4):
                bank = 4 + (i0 // 4) % 2
                pv = PS[:, bank * 512:(bank + 1) * 512]
                b.wait(te, "VE")
                for qq in range(4):
                    tc = i0 + qq
                    last = te.transpose(pv[:, qq * 128:(qq + 1) * 128],
                                        tok[:, tc * 128:(tc + 1) * 128], ident[:])
                b.inc(last, "TE")
                b.wait(ve, "TE")
                ins = ve.tensor_copy(qr[:, i0 * 128:i0 * 128 + 512], pv)
                b.inc(ins, "VE")
            b.wait(se, "VE")
            b.dma(se, qkvs[srow], qr, "LG")

    def emit_masks():
        mk = bigv(C["MASKS"], 2048)
        for tqc in range(4):
            b.wait(gp, "VE")
            ins = gp.iota(ioti[:], [[1, 512]], base=-tqc * 128, channel_multiplier=-1)
            b.inc(ins, "GE")
            b.wait(ve, "GE")
            mv = mk[:, tqc * 512:(tqc + 1) * 512]
            ve.tensor_copy(mv, ioti[:])
            ins = ve.tensor_scalar(mv, mv, 0.0, NEG, ALU.is_gt, ALU.mult)
            b.inc(ins, "VE")
            b.wait(ve, "VE")

    def emit_attn_core():
        mk = bigv(C["MASKS"], 2048)
        sc_rel = [0]
        av_rel = [0]
        for g in range(NKV):
            kt = bigv(C["KT"], 2048)
            vt = bigv(C["VT"], 2048)
            vg = bigv(C["VG"], 2048)
            b.wait(sy, "TE")
            b.dma(sy, kt, qkvs[16 + g], "LD")
            b.dma(sy, vt, qkvs[20 + g], "LD")
            b.wait(te, "LD")
            for i0 in range(0, 16, 4):
                pv = PS[:, 3584:4096]
                b.wait(te, "VE")
                for qq in range(4):
                    kc = i0 + qq
                    last = te.transpose(pv[:, qq * 128:(qq + 1) * 128],
                                        vt[:, kc * 128:(kc + 1) * 128], ident[:])
                b.inc(last, "TE")
                b.wait(ve, "TE")
                ins = ve.tensor_copy(vg[:, i0 * 128:i0 * 128 + 512], pv)
                b.inc(ins, "VE")
            for qh in range(4):
                h = g * 4 + qh
                qt = bigv(C["QT"], 2048)
                b.wait(sy, "TE")
                b.dma(sy, qt, qkvs[h], "LD")
                b.wait(te, "LD")
                aot = bigv(C["AOT"], 2048)
                for tqg in range(4):
                    nb = tqg + 1
                    for tqc in range(4):
                        tq0 = tqg * 512 + tqc * 128
                        b.wait(te, "VE", sc_rel[0])
                        for kb in range(nb):
                            pv = PS[:, kb * 512:(kb + 1) * 512]
                            ins = te.matmul(pv, qt[:, tq0:tq0 + 128],
                                            kt[:, kb * 512:(kb + 1) * 512],
                                            start=True, stop=True)
                        b.inc(ins, "TE")
                        ss = bigv(C["SS"], 2048)
                        pp = bigv(C["PP"], 2048)
                        b.wait(ve, "TE")
                        ins = ve.tensor_copy(ss[:, 0:nb * 512], PS[:, 0:nb * 512])
                        b.inc(ins, "VE")
                        sc_rel[0] = b.c["VE"]
                        dia = ss[:, (nb - 1) * 512: nb * 512]
                        ve.tensor_tensor(dia, dia, mk[:, tqc * 512:(tqc + 1) * 512],
                                         ALU.add)
                        ins = ve.reduce_max(small[:, 0:1], ss[:, 0:nb * 512], AX.X)
                        b.inc(ins, "VE")
                        b.wait(ve, "VE")
                        ins = ve.tensor_scalar(small[:, 1:2], small[:, 0:1], -CSC,
                                               None, ALU.mult)
                        b.inc(ins, "VE")
                        b.wait(se, "VE")
                        ins = se.activation(pp[:, 0:nb * 512], ss[:, 0:nb * 512],
                                            AF.Exp, bias=small[:, 1:2], scale=CSC)
                        b.inc(ins, "SE")
                        b.wait(ve, "SE")
                        ins = ve.reduce_sum(small[:, 2:3], pp[:, 0:nb * 512], AX.X)
                        b.inc(ins, "VE")
                        b.wait(ve, "VE")
                        ins = ve.reciprocal(small[:, 3:4], small[:, 2:3])
                        b.inc(ins, "VE")
                        b.wait(ve, "VE")
                        ins = ve.tensor_scalar(pp[:, 0:nb * 512], pp[:, 0:nb * 512],
                                               small[:, 3:4], None, ALU.mult)
                        b.inc(ins, "VE")
                        b.wait(te, "VE")
                        for k0 in range(0, nb * 4, 4):
                            bank = 4 + (k0 // 4) % 2
                            pv = PS[:, bank * 512:(bank + 1) * 512]
                            b.wait(te, "VE")
                            for qq in range(4):
                                kc = k0 + qq
                                last = te.transpose(pv[:, qq * 128:(qq + 1) * 128],
                                                    pp[:, kc * 128:(kc + 1) * 128],
                                                    ident[:])
                            b.inc(last, "TE")
                            b.wait(ve, "TE")
                            dstv = r3(BIG[:, C["PT"] + k0 * 512:
                                          C["PT"] + (k0 + 4) * 512], 512)[
                                :, :, tqc * 128: tqc * 128 + 128]
                            ins = ve.tensor_copy(dstv, r3(pv, 128))
                            b.inc(ins, "VE")
                    pv = PS[:, 3072:3584]
                    b.wait(te, "VE")
                    b.wait(te, "VE", av_rel[0])
                    for kc in range(nb * 4):
                        last = te.matmul(pv, vg[:, kc * 128:(kc + 1) * 128],
                                         bigv(C["PT"] + kc * 512, 512),
                                         start=(kc == 0), stop=(kc == nb * 4 - 1))
                    b.inc(last, "TE")
                    b.wait(ve, "TE")
                    ins = ve.tensor_copy(aot[:, tqg * 512:(tqg + 1) * 512], pv)
                    b.inc(ins, "VE")
                    av_rel[0] = b.c["VE"]
                b.wait(se, "VE")
                b.dma(se, aos[h], aot, "LG")

    def emit_oproj():
        borow = bigv(O["XR0"], 2048)
        b.wait(se, "LG")
        b.dma(se, borow[0:1, :], bo_in.rearrange("(o a) -> o a", o=1), "LG")
        b.wait(te, "LG")
        b.wait(te, "VE")
        for sgl in range(4):
            pv = PS[:, sgl * 512:(sgl + 1) * 512]
            ins = te.matmul(pv, ones1[:], borow[0:1, sgl * 512:(sgl + 1) * 512],
                            start=True, stop=True)
        b.inc(ins, "TE")
        b.wait(ve, "TE")
        ins = ve.tensor_copy(bigv(O["BOREP"], 2048), PS[:, 0:2048])
        b.inc(ins, "VE")
        for ec in range(16):
            b.dma(sy, bigv(O["AOTF"] + ec * 2048, 2048), aos[ec], "LD")
        aotf_ld = b.c["LD"]
        wot_rel = {}
        for tcg in range(8):
            for ec in range(16):
                p = ec % 2
                if p in wot_rel:
                    b.wait(sy, "TE", wot_rel[p])
                wot = bigv(O[f"WOT{p}"], 2048)
                b.dma(sy, wot, oTw[ec], "LD")
                b.wait(te, "LD")
                if ec == 0:
                    b.wait(te, "LD", aotf_ld)
                    b.wait(te, "VE")
                for tc2 in range(2):
                    tc = tcg * 2 + tc2
                    for dg in range(4):
                        pv = PS[:, tc2 * 2048 + dg * 512: tc2 * 2048 + (dg + 1) * 512]
                        last = te.matmul(
                            pv,
                            bigv(O["AOTF"] + ec * 2048 + tc * 128, 128),
                            wot[:, dg * 512:(dg + 1) * 512],
                            start=(ec == 0), stop=(ec == 15))
                b.inc(last, "TE")
                wot_rel[p] = b.c["TE"]
            for tc2 in range(2):
                tc = tcg * 2 + tc2
                t0 = tc * 128
                xr = bigv(O[f"XR{tc2}"], 2048)
                b.wait(se, "VE")
                b.wait(se, "LG")
                b.dma(se, xr, x1s[t0:t0 + 128, :], "LG")
                b.wait(ve, "TE")
                b.wait(ve, "LG")
                pv = PS[:, tc2 * 2048:(tc2 + 1) * 2048]
                ve.tensor_tensor(xr, xr, pv, ALU.add)
                ins = ve.tensor_tensor(xr, xr, bigv(O["BOREP"], 2048), ALU.add)
                b.inc(ins, "VE")
                b.wait(se, "VE")
                b.dma(se, x2s[t0:t0 + 128, :], xr, "LG")

    build_rope_tables()
    emit_qkv()
    b.barrier()
    emit_masks()
    emit_attn_core()
    b.barrier()
    emit_oproj()
    b.barrier()
    if stop == "attn":
        b.wait(se, "LG")
        ctx.close()
        return nc

    emit_ffn(1, x2s, y_out)
    b.wait(se, "LG")
    ctx.close()
    return nc


def prep_inputs(inputs):
    x = np.ascontiguousarray(np.asarray(inputs["L0_attn_x"], dtype=np.float32))
    w = {
        "wq": inputs["L0_attn_Wq"], "bq": inputs["L0_attn_bq"],
        "wk": inputs["L0_attn_Wk"], "bk": inputs["L0_attn_bk"],
        "wv": inputs["L0_attn_Wv"], "bv": inputs["L0_attn_bv"],
        "wo": inputs["L0_attn_Wo"], "bo": inputs["L0_attn_bo"],
        "qg": inputs["L0_attn_q_gain"], "rw": inputs["L0_ffn_router_w"],
        "gw": inputs["L0_ffn_gate_w"], "uw": inputs["L0_ffn_up_w"],
        "dw": inputs["L0_ffn_down_w"],
    }
    w = {k: np.ascontiguousarray(np.asarray(v, dtype=np.float32))
         for k, v in w.items()}
    return x, w


def kernel(**inputs):
    x, w = prep_inputs(inputs)
    nc = build_nc(stop="full")
    in_maps = [dict(x=x[i], **w) for i in range(8)]
    res = run_bass_kernel_spmd(nc, in_maps, core_ids=list(range(8)))
    return np.stack([r["y"] for r in res.results]).astype(np.float32)


# version 03 (final)
# speedup vs baseline: 1.1205x; 1.1205x over previous
import math
from contextlib import ExitStack

import numpy as np
import concourse.bass as bass
import concourse.mybir as mybir
from concourse.bass_utils import run_bass_kernel_spmd

F32 = mybir.dt.float32
BF16 = mybir.dt.bfloat16
I32 = mybir.dt.int32
AF = mybir.ActivationFunctionType
ALU = mybir.AluOpType
AX = mybir.AxisListType

S = 2048
Dm = 2048
IFF = 4096
NE = 8
NH = 16
NKV = 4
HD = 128
NQ = 4
TQ = 512
IG = 4
NIC = IFF // 128
NDC = Dm // 128
LNEPS = 1e-5
RMSEPS = 1.1920929e-07
TOPP = 0.9
CSC = 1.0 / math.sqrt(HD)
NEG = -1e30
TWO_PI_HI = float(np.float32(2 * math.pi))
TWO_PI_LO = float(np.float64(2 * math.pi) - np.float64(np.float32(2 * math.pi)))


def rope_host_tables():
    t = (np.arange(16)[None, :, None] * 128 + np.arange(128)[:, None, None])
    j = np.arange(64)[None, None, :]
    inv = 1.0 / 10000.0 ** (j / 64.0)
    fr = (t * inv).astype(np.float64)
    ks = np.round(fr / (2 * np.pi))
    kc = np.round((fr + np.pi / 2) / (2 * np.pi))
    return (ks.reshape(128, 1024).astype(np.float32),
            kc.reshape(128, 1024).astype(np.float32))


def r3(ap, blk):
    return ap.rearrange("p (a b) -> p a b", b=blk)


def build_nc(stop="full", dbg=()):
    nc = bass.Bass("TRN2", target_bir_lowering=False, debug=False)
    ctx = ExitStack()

    x_in = nc.dram_tensor("x", [S, Dm], F32, kind="ExternalInput").ap()
    wq_in = nc.dram_tensor("wq", [Dm, Dm], F32, kind="ExternalInput").ap()
    bq_in = nc.dram_tensor("bq", [Dm], F32, kind="ExternalInput").ap()
    wk_in = nc.dram_tensor("wk", [512, Dm], F32, kind="ExternalInput").ap()
    bk_in = nc.dram_tensor("bk", [512], F32, kind="ExternalInput").ap()
    wv_in = nc.dram_tensor("wv", [512, Dm], F32, kind="ExternalInput").ap()
    bv_in = nc.dram_tensor("bv", [512], F32, kind="ExternalInput").ap()
    wo_in = nc.dram_tensor("wo", [Dm, Dm], F32, kind="ExternalInput").ap()
    bo_in = nc.dram_tensor("bo", [Dm], F32, kind="ExternalInput").ap()
    qg_in = nc.dram_tensor("qg", [NH], F32, kind="ExternalInput").ap()
    rw_in = nc.dram_tensor("rw", [NE, Dm], F32, kind="ExternalInput").ap()
    gw_in = nc.dram_tensor("gw", [NE, IFF, Dm], F32, kind="ExternalInput").ap()
    uw_in = nc.dram_tensor("uw", [NE, IFF, Dm], F32, kind="ExternalInput").ap()
    dw_in = nc.dram_tensor("dw", [NE, Dm, IFF], F32, kind="ExternalInput").ap()
    y_out = nc.dram_tensor("y", [S, Dm], F32, kind="ExternalOutput").ap()

    def scrt(name, shape):
        kind = "ExternalOutput" if name in dbg else "Internal"
        return nc.dram_tensor(name, shape, F32, kind=kind).ap()

    def scrtb(name, shape):
        kind = "ExternalOutput" if name in dbg else "Internal"
        return nc.dram_tensor(name, shape, BF16, kind=kind).ap()
    gTh = scrtb("gTh", [NE, NIC, 128, Dm])
    gTl = scrtb("gTl", [NE, NIC, 128, Dm])
    uTh = scrtb("uTh", [NE, NIC, 128, Dm])
    uTl = scrtb("uTl", [NE, NIC, 128, Dm])
    dTh = scrtb("dTh", [NE, NIC, 128, Dm])
    dTl = scrtb("dTl", [NE, NIC, 128, Dm])
    qTw = scrt("qTw", [16, 128, Dm])
    kTw = scrt("kTw", [4, 128, Dm])
    vTw = scrt("vTw", [4, 128, Dm])
    oTw = scrt("oTw", [16, 128, Dm])
    rTw = scrt("rTw", [128, 128])
    xlnfd = scrt("xlnfd", [128, 32768])
    x1s = scrt("x1s", [S, Dm])
    x2s = scrt("x2s", [S, Dm])
    qkvs = scrt("qkvs", [24, 128, S])
    aos = scrt("aos", [16, 128, S])

    ks_np, kc_np = rope_host_tables()
    ksd = nc.inline_tensor(ks_np, name="ks_d").ap()
    kcd = nc.inline_tensor(kc_np, name="kc_d").ap()
    idn = nc.inline_tensor(np.eye(128, dtype=np.float32), name="idn_d").ap()

    ident = nc.alloc_sbuf_tensor("ident", [128, 128], F32).ap()
    ones1 = nc.alloc_sbuf_tensor("ones1", [1, 128], F32).ap()
    qgr = nc.alloc_sbuf_tensor("qgr", [128, 16], F32).ap()
    biasq = nc.alloc_sbuf_tensor("biasq", [128, 16], F32).ap()
    biask = nc.alloc_sbuf_tensor("biask", [128, 4], F32).ap()
    biasv = nc.alloc_sbuf_tensor("biasv", [128, 4], F32).ap()
    stats = nc.alloc_sbuf_tensor("stats", [128, 32], F32).ap()
    small = nc.alloc_sbuf_tensor("small", [128, 128], F32).ap()
    ioti = nc.alloc_sbuf_tensor("ioti", [128, 512], I32).ap()
    cosT = nc.alloc_sbuf_tensor("cosT", [128, 1024], F32).ap()
    sinT = nc.alloc_sbuf_tensor("sinT", [128, 1024], F32).ap()
    rt_sb = nc.alloc_sbuf_tensor("rt_sb", [128, 128], F32).ap()
    rthl = nc.alloc_sbuf_tensor("rthl", [128, 256], BF16).ap()
    BIGC = 45184
    BIGt = nc.alloc_sbuf_tensor("BIG", [128, BIGC], F32)
    BIG = BIGt.ap()
    big_addr = nc.lookup_mloc(BIGt).addr
    BFOFF = 16512
    BFC = (BIGC - BFOFF) * 2
    PSg = ctx.enter_context(nc.psum_tensor("PSb", [128, 4096], F32))
    PS = PSg

    def bigv(off, n):
        return BIG[:, off:off + n]



    def mklay(*items):
        d, off = {}, 0
        for name, n in items:
            d[name] = off
            off += n
        assert off <= BIGC, (off, items[0])
        return d

    L = mklay(("ACC", 8192), ("WREPA", 4096), ("LNX0", 2048),
              ("GSB0", 512), ("GSB1", 512), ("HTMP", 512),
              ("WROW", 512), ("WALL", 32))
    assert L["WALL"] + 32 <= BFOFF
    FBT = {}
    off2 = 0
    for name, n in [("XLH", 8192), ("XLL", 8192),
                    ("WGH0", 2048), ("WGH1", 2048), ("WGL0", 2048), ("WGL1", 2048),
                    ("WUH0", 2048), ("WUH1", 2048), ("WUL0", 2048), ("WUL1", 2048),
                    *[(f"WDH{k}", 2048) for k in range(IG)],
                    *[(f"WDL{k}", 2048) for k in range(IG)],
                    ("HBH", 2048), ("HBL", 2048)]:
        FBT[name] = nc.alloc_sbuf_tensor_at(
            "bf_" + name, [128, n], BF16,
            offset=big_addr + BFOFF * 4 + off2 * 2).ap()
        off2 += n
    assert off2 <= BFC, (off2, BFC)
    P0_BF_MIN = (32768 - BFOFF) * 2
    for k, coff in enumerate([32768, 34816, 36864, 38912]):
        assert coff >= P0_BF_MIN
        nm = ("CONVH0", "CONVL0", "CONVH1", "CONVL1")[k]
        FBT[nm] = nc.alloc_sbuf_tensor_at(
            "bf_" + nm, [128, 2048], BF16,
            offset=big_addr + BFOFF * 4 + coff * 2).ap()
    A = mklay(("XLNF", 32768), ("WPJ0", 2048), ("WPJ1", 2048),
              ("QR", 2048), ("TOK", 2048), ("TMPA", 2048), ("TMPB", 2048))
    C = mklay(("KT", 2048), ("VT", 2048), ("VG", 2048), ("QT", 2048),
              ("SS", 2048), ("PP", 2048), ("PT", 8192), ("AOT", 2048),
              ("MASKS", 2048))
    O = mklay(("AOTF", 32768), ("WOT0", 2048), ("WOT1", 2048),
              ("BOREP", 2048), ("XR0", 2048), ("XR1", 2048))
    P0 = mklay(("NAT0", 8192), ("NAT1", 8192),
               *[(f"ASM{k}", 2048) for k in range(8)])

    class Bld:
        def __init__(self):
            self.sm, self.c = {}, {}
            for nm in ["LD", "LG", "TE", "VE", "SE", "GE", "BR",
                       "DF", "SIL", "HH", "FL", "MG", "MU", "MD",
                       "W0", "W1", "D0", "LA", "LB", "LX", "G0", "G1",
                       "N0", "N1"]:
                self.sm[nm] = ctx.enter_context(nc.semaphore("s" + nm))
                self.c[nm] = 0

        def inc(self, ins, s, n=1):
            ins.then_inc(self.sm[s], n)
            self.c[s] += n
            return self.c[s]

        def dma(self, eng, out, in_, s):
            return self.inc(eng.dma_start(out=out, in_=in_), s, 16)

        def wait(self, eng, s, v=None):
            v = self.c[s] if v is None else v
            if v > 0:
                eng.wait_ge(self.sm[s], v)

        def barrier(self):
            engs = [nc.sync, nc.tensor, nc.vector, nc.scalar, nc.gpsimd]
            for e in engs:
                self.inc(e.nop(), "BR", 1)
            v = self.c["BR"]
            for e in engs:
                e.wait_ge(self.sm["BR"], v)

    b = Bld()
    sy, te, ve, se, gp = nc.sync, nc.tensor, nc.vector, nc.scalar, nc.gpsimd

    b.dma(sy, ident[:], idn[:], "LD")
    b.wait(ve, "LD")
    ins = ve.memset(ones1[:], 1.0)
    b.inc(ins, "VE")

    nat_rel = []
    slab_i = [0]
    asm_free = {}
    conv_free = {}
    conv_cnt = [0]
    p0_bank = {}

    def do_slab(dmf, nsub, ninner, col_fn, outs, tile_cols, nat_part=128,
                bf_outs=None):
        si = slab_i[0]
        slab_i[0] += 1
        nsem = f"N{si % 2}"
        natv = bigv(P0[f"NAT{si % 2}"], 8192)
        if si >= 2:
            b.wait(sy, "TE", nat_rel[si - 2])
        dmf(natv, sem=nsem)
        b.wait(te, nsem)
        for j in range(nsub):
            slot = (si % 2) * 4 + j
            av = bigv(P0[f"ASM{slot}"], 2048)
            if slot in asm_free:
                sm_, vv_ = asm_free[slot]
                b.wait(se, sm_, vv_)
            for i0 in range(0, ninner, 4):
                bank = (i0 // 4) % 4
                pv = PS[:, bank * 512:(bank + 1) * 512]
                b.wait(te, "SE", p0_bank.get(bank, 0))
                last = None
                for q in range(4):
                    i = i0 + q
                    if i >= ninner:
                        break
                    src = natv[0:nat_part, col_fn(j, i)]
                    w = src.shape[-1]
                    last = te.transpose(pv[0:w, q * nat_part:q * nat_part + nat_part],
                                        src, ident[:])
                b.inc(last, "TE")
                b.wait(se, "TE")
                n = min(4, ninner - i0) * nat_part
                ins = se.copy(av[:, i0 * nat_part:i0 * nat_part + n], pv[:, 0:n])
                b.inc(ins, "SE")
                p0_bank[bank] = b.c["SE"]
            b.wait(se, "SE")
            b.inc(se.nop(), "SE")
            if bf_outs is None:
                b.dma(se, outs[j], av[:, 0:tile_cols], "LG")
                asm_free[slot] = ("LG", b.c["LG"])
            else:
                p = conv_cnt[0] % 2
                use_gp = False
                conv_cnt[0] += 1
                ch = FBT[f"CONVH{p}"]
                cl = FBT[f"CONVL{p}"]
                if p in conv_free:
                    b.wait(ve, f"G{p}")
                b.wait(ve, "SE")
                ins = ve.tensor_copy(ch, av[:, 0:2048])
                b.inc(ins, "VE")
                b.wait(ve, "VE")
                if use_gp:
                    b.wait(gp, "VE")
                    ins = gp.tensor_tensor(cl, av[:, 0:2048], ch, ALU.subtract)
                    b.inc(ins, "GE")
                    b.wait(se, "GE")
                    b.wait(se, "VE")
                else:
                    ins = ve.tensor_tensor(cl, av[:, 0:2048], ch, ALU.subtract)
                    b.inc(ins, "VE")
                    b.wait(ve, "VE")
                    b.inc(ve.nop(), "VE")
                    b.wait(se, "VE")
                hi_t, lo_t = bf_outs[j]
                b.dma(se, hi_t, ch, f"G{p}")
                b.dma(se, lo_t, cl, f"G{p}")
                conv_free[p] = 1
                asm_free[slot] = ("VE", b.c["VE"]) if not use_gp else ("GE", b.c["GE"])
        nat_rel.append(b.c["TE"])

    def image_weight(w_nat, O_rows, tiles_out, bf_tiles=None):
        for s_i in range(O_rows // 512):
            def dmf(natv, s_i=s_i, sem="N0"):
                for j in range(4):
                    b.dma(sy, natv[:, j * 2048:(j + 1) * 2048],
                          w_nat[s_i * 512 + j * 128: s_i * 512 + (j + 1) * 128, :],
                          sem)
            do_slab(dmf, 4, 16,
                    lambda j, i: slice(j * 2048 + i * 128, j * 2048 + (i + 1) * 128),
                    [tiles_out[s_i * 4 + j] for j in range(4)] if tiles_out else None,
                    2048,
                    bf_outs=None if bf_tiles is None else
                    [bf_tiles[s_i * 4 + j] for j in range(4)])

    def plain_weight(w_nat, I_cols, tiles_out, bf_tiles=None):
        for s_i in range(I_cols // 512):
            def dmf(natv, s_i=s_i, sem="N0"):
                src = w_nat[:, s_i * 512:(s_i + 1) * 512].rearrange(
                    "(a p) c -> p a c", p=128)
                b.dma(sy, r3(natv, 512), src, sem)
            do_slab(dmf, 4, 16,
                    lambda j, i: slice(i * 512 + j * 128, i * 512 + j * 128 + 128),
                    [tiles_out[s_i * 4 + j] for j in range(4)] if tiles_out else None,
                    2048,
                    bf_outs=None if bf_tiles is None else
                    [bf_tiles[s_i * 4 + j] for j in range(4)])

    for e in range(NE):
        image_weight(gw_in[e], IFF, None,
                     bf_tiles=[(gTh[e, ic], gTl[e, ic]) for ic in range(NIC)])
        image_weight(uw_in[e], IFF, None,
                     bf_tiles=[(uTh[e, ic], uTl[e, ic]) for ic in range(NIC)])
        plain_weight(dw_in[e], IFF, None,
                     bf_tiles=[(dTh[e, ic], dTl[e, ic]) for ic in range(NIC)])
    image_weight(wq_in, Dm, [qTw[i] for i in range(16)])
    image_weight(wk_in, 512, [kTw[i] for i in range(4)])
    image_weight(wv_in, 512, [vTw[i] for i in range(4)])
    plain_weight(wo_in, Dm, [oTw[i] for i in range(16)])
    si = slab_i[0]
    slab_i[0] += 1
    natv = bigv(P0[f"NAT{si % 2}"], 8192)
    if si >= 2:
        b.wait(sy, "TE", nat_rel[si - 2])
    b.dma(sy, natv[0:8, 0:2048], rw_in[:, :], f"N{si % 2}")
    b.wait(te, f"N{si % 2}")
    av = bigv(P0[f"ASM{(si % 2) * 4}"], 2048)
    for i0 in range(0, 16, 4):
        bank = (i0 // 4) % 2
        pv = PS[:, bank * 512:(bank + 1) * 512]
        b.wait(te, "VE")
        b.wait(te, "SE")
        for q in range(4):
            dc = i0 + q
            last = te.transpose(pv[:, q * 8:(q + 1) * 8],
                                natv[0:8, dc * 128:(dc + 1) * 128], ident[0:8, 0:8])
        b.inc(last, "TE")
        b.wait(ve, "TE")
        ins = ve.tensor_copy(av[:, i0 * 8:i0 * 8 + 32], pv[:, 0:32])
        b.inc(ins, "VE")
    b.wait(se, "VE")
    b.dma(se, rTw[:, :], av[:, 0:128], "LG")
    nat_rel.append(b.c["TE"])

    b.barrier()
    if stop == "p0":
        b.wait(se, "LG")
        b.wait(se, "G0")
        b.wait(se, "G1")
        ctx.close()
        return nc

    lnx_rel = {}
    acc_store = [0]
    gate_rel = {}
    up_rel = {}
    wbuf_rel = {}
    dgrp = [0]
    grp_fl = {}
    fl_rel = {}
    gi_ctr = [0]

    b.wait(sy, "G0")
    b.wait(sy, "G1")
    b.wait(sy, "LG")
    b.dma(sy, rt_sb[:], rTw[:, :], "LD")
    rt_ld = b.c["LD"]
    b.wait(ve, "LD", rt_ld)
    ins = ve.tensor_copy(rthl[:, 0:128], rt_sb[:])
    b.inc(ins, "VE")
    b.wait(ve, "VE")
    ins = ve.tensor_tensor(rthl[:, 128:256], rt_sb[:], rthl[:, 0:128], ALU.subtract)
    b.inc(ins, "VE")
    b.wait(ve, "VE")
    rt_split = b.c["VE"]

    def emit_ln_quarter(src, q, dest_off, dest_blk, split=False, lx_off=None):
        for tc in range(4):
            t0 = q * TQ + tc * 128
            lx = bigv(L["LNX0"] if lx_off is None else lx_off, 2048)
            key = 0
            if key in lnx_rel:
                b.wait(se, "TE", lnx_rel[key])
            b.wait(se, "VE")
            b.dma(se, lx, src[t0:t0 + 128, :], "LX")
            b.wait(ve, "LX")
            for i in range(4):
                ins = ve.bn_stats(stats[:, i * 6:(i + 1) * 6], lx[:, i * 512:(i + 1) * 512])
            b.inc(ins, "VE")
            b.wait(ve, "VE")
            ins = ve.bn_aggr(stats[:, 24:26], stats[:, 0:24])
            b.inc(ins, "VE")
            b.wait(ve, "VE")
            ins = ve.tensor_scalar(small[:, 0:1], stats[:, 25:26], LNEPS, None, ALU.add)
            b.inc(ins, "VE")
            b.wait(se, "VE")
            ins = se.activation(small[:, 1:2], small[:, 0:1], AF.Sqrt)
            b.inc(ins, "SE")
            b.wait(ve, "SE")
            ins = ve.reciprocal(small[:, 2:3], small[:, 1:2])
            b.inc(ins, "VE")
            b.wait(ve, "VE")
            ins = ve.tensor_scalar(lx, lx, stats[:, 24:25], small[:, 2:3],
                                   ALU.subtract, ALU.mult)
            b.inc(ins, "VE")
            b.wait(te, "VE")
            for i0 in range(0, 16, 4):
                bank = 2 + (i0 // 4) % 2
                pv = PS[:, bank * 512:(bank + 1) * 512]
                b.wait(te, "VE")
                for qq in range(4):
                    dc = i0 + qq
                    last = te.transpose(pv[:, qq * 128:(qq + 1) * 128],
                                        lx[:, dc * 128:(dc + 1) * 128], ident[:])
                b.inc(last, "TE")
                b.wait(ve, "TE")
                if not split:
                    toff = (q * TQ if dest_blk == 2048 else 0) + tc * 128
                    dst = r3(BIG[:, dest_off + i0 * dest_blk:
                                 dest_off + (i0 + 4) * dest_blk], dest_blk)[
                        :, :, toff: toff + 128]
                    ins = ve.tensor_copy(dst, r3(pv, 128))
                    b.inc(ins, "VE")
                else:
                    toff = tc * 128
                    dsth = r3(FBT["XLH"][:, i0 * 512:(i0 + 4) * 512], 512)[
                        :, :, toff: toff + 128]
                    dstl = r3(FBT["XLL"][:, i0 * 512:(i0 + 4) * 512], 512)[
                        :, :, toff: toff + 128]
                    ins = ve.tensor_copy(dsth, r3(pv, 128))
                    b.inc(ins, "VE")
                    b.wait(ve, "VE")
                    ins = ve.tensor_tensor(dstl, r3(pv, 128), dsth, ALU.subtract)
                    b.inc(ins, "VE")
                    b.wait(ve, "VE")
            lnx_rel[key] = b.c["TE"]

    def emit_router(q):
        wall = bigv(L["WALL"], 32)
        for tc in range(4):
            pv = PS[:, 2048:2056]
            b.wait(te, "VE")
            b.wait(te, "VE", rt_split)
            b.wait(te, "FL")
            for dc in range(NDC):
                xh = FBT["XLH"][:, dc * 512 + tc * 128: dc * 512 + tc * 128 + 128]
                xl = FBT["XLL"][:, dc * 512 + tc * 128: dc * 512 + tc * 128 + 128]
                rh = rthl[:, dc * 8:dc * 8 + 8]
                rl = rthl[:, 128 + dc * 8:128 + dc * 8 + 8]
                te.matmul(pv, xh, rh, start=(dc == 0), stop=False)
                te.matmul(pv, xl, rh, start=False, stop=False)
                last = te.matmul(pv, xh, rl, start=False,
                                 stop=(dc == NDC - 1))
            b.inc(last, "TE")
            b.wait(ve, "TE")
            pr = small[:, 8:16]
            ins = ve.tensor_copy(small[:, 16:24], pv[:, 0:8])
            b.inc(ins, "VE")
            b.wait(ve, "VE")
            ins = ve.reduce_max(small[:, 0:1], small[:, 16:24], AX.X)
            b.inc(ins, "VE")
            b.wait(ve, "VE")
            ins = ve.tensor_scalar(small[:, 1:2], small[:, 0:1], -1.0, None, ALU.mult)
            b.inc(ins, "VE")
            b.wait(se, "VE")
            ins = se.activation(pr, small[:, 16:24], AF.Exp, bias=small[:, 1:2],
                                scale=1.0)
            b.inc(ins, "SE")
            b.wait(ve, "SE")
            ins = ve.reduce_sum(small[:, 2:3], pr, AX.X)
            b.inc(ins, "VE")
            b.wait(ve, "VE")
            ins = ve.reciprocal(small[:, 3:4], small[:, 2:3])
            b.inc(ins, "VE")
            b.wait(ve, "VE")
            ins = ve.tensor_scalar(pr, pr, small[:, 3:4], None, ALU.mult)
            b.inc(ins, "VE")
            b.wait(ve, "VE")
            for e in range(NE):
                ins = ve.scalar_tensor_tensor(small[:, 24:32], pr, pr[:, e:e + 1], pr,
                                              ALU.is_gt, ALU.mult)
                b.inc(ins, "VE")
                b.wait(ve, "VE")
                ins = ve.reduce_sum(small[:, 4:5], small[:, 24:32], AX.X)
                b.inc(ins, "VE")
                b.wait(ve, "VE")
                ins = ve.tensor_scalar(small[:, 5:6], small[:, 4:5], TOPP, None, ALU.is_le)
                b.inc(ins, "VE")
                b.wait(ve, "VE")
                ins = ve.tensor_tensor(wall[:, tc * 8 + e: tc * 8 + e + 1],
                                       small[:, 5:6], pr[:, e:e + 1], ALU.mult)
            b.inc(ins, "VE")

    def emit_wrep_all(q):
        wall = bigv(L["WALL"], 32)
        wrow = bigv(L["WROW"], 512)
        for e in range(NE):
            pv = PS[:, 3584:4096]
            b.wait(te, "VE")
            b.wait(te, "FL")
            for tc in range(4):
                last = te.transpose(pv[0:1, tc * 128:(tc + 1) * 128],
                                    wall[:, tc * 8 + e: tc * 8 + e + 1], ident[:])
            b.inc(last, "TE")
            b.wait(ve, "TE")
            ins = ve.tensor_copy(wrow[0:1, :], pv[0:1, :])
            b.inc(ins, "VE")
            b.wait(te, "VE")
            ins = te.matmul(pv, ones1[:], wrow[0:1, :], start=True, stop=True)
            b.inc(ins, "TE")
            b.wait(ve, "TE")
            ins = ve.tensor_copy(bigv(L["WREPA"] + e * 512, 512), pv)
            b.inc(ins, "VE")
        b.wait(te, "VE")

    def emit_ffn(call, src, dst):
        for q in range(NQ):
            b.wait(ve, "MU")
            b.wait(ve, "TE")
            emit_ln_quarter(src, q, 0, 512, split=True)
            emit_router(q)
            emit_wrep_all(q)
            for e in range(NE):
                for ig in range(NIC // IG):
                    g = dgrp[0]
                    for il in range(IG):
                        i = ig * IG + il
                        gi = gi_ctr[0]
                        gi_ctr[0] += 1
                        p = gi % 2
                        if gi >= 2:
                            b.wait(sy, "MU", wbuf_rel.get(p, 0))
                        wgh = FBT[f"WGH{p}"]
                        wgl = FBT[f"WGL{p}"]
                        wuh = FBT[f"WUH{p}"]
                        wul = FBT[f"WUL{p}"]
                        b.dma(sy, wgh, gTh[e, i], f"W{p}")
                        b.dma(sy, wgl, gTl[e, i], f"W{p}")
                        b.dma(sy, wuh, uTh[e, i], f"W{p}")
                        b.dma(sy, wul, uTl[e, i], f"W{p}")
                        wgu_done = b.c[f"W{p}"]
                        if g >= 1:
                            b.wait(sy, "FL", grp_fl[g - 1])
                        b.dma(sy, FBT[f"WDH{il}"], dTh[e, i], "D0")
                        b.dma(sy, FBT[f"WDL{il}"], dTl[e, i], "D0")
                        pg = PS[:, p * 512:(p + 1) * 512]
                        pu = PS[:, 1024 + p * 512:1024 + (p + 1) * 512]
                        b.wait(te, f"W{p}", wgu_done)
                        if gi >= 2:
                            b.wait(te, "SIL", gate_rel.get(p, 0))
                        for dc in range(NDC):
                            xh = FBT["XLH"][:, dc * 512:(dc + 1) * 512]
                            xl = FBT["XLL"][:, dc * 512:(dc + 1) * 512]
                            wh = wgh[:, dc * 128:(dc + 1) * 128]
                            wl = wgl[:, dc * 128:(dc + 1) * 128]
                            te.matmul(pg, wh, xh, start=(dc == 0), stop=False)
                            te.matmul(pg, wh, xl, start=False, stop=False)
                            last = te.matmul(pg, wl, xh, start=False,
                                             stop=(dc == NDC - 1))
                        b.inc(last, "MG")
                        if gi >= 2:
                            b.wait(te, "HH", up_rel.get(p, 0))
                        for dc in range(NDC):
                            xh = FBT["XLH"][:, dc * 512:(dc + 1) * 512]
                            xl = FBT["XLL"][:, dc * 512:(dc + 1) * 512]
                            wh = wuh[:, dc * 128:(dc + 1) * 128]
                            wl = wul[:, dc * 128:(dc + 1) * 128]
                            te.matmul(pu, wh, xh, start=(dc == 0), stop=False)
                            te.matmul(pu, wh, xl, start=False, stop=False)
                            last = te.matmul(pu, wl, xh, start=False,
                                             stop=(dc == NDC - 1))
                        b.inc(last, "MU")
                        wbuf_rel[p] = b.c["MU"]
                        gsb = bigv(L[f"GSB{p}"], 512)
                        b.wait(se, "MG")
                        if gi >= 2:
                            b.wait(se, "HH", up_rel.get(p, 0))
                        ins = se.activation(gsb, pg, AF.Silu)
                        b.inc(ins, "SIL")
                        gate_rel[p] = b.c["SIL"]
                        htmp = bigv(L["HTMP"], 512)
                        hbh = FBT["HBH"][:, il * 512:(il + 1) * 512]
                        hbl = FBT["HBL"][:, il * 512:(il + 1) * 512]
                        b.wait(ve, "SIL")
                        b.wait(ve, "MU")
                        if g >= 1:
                            b.wait(ve, "FL", grp_fl[g - 1])
                        ve.tensor_tensor(htmp, gsb, pu, ALU.mult)
                        ve.tensor_tensor(htmp, htmp, bigv(L["WREPA"] + e * 512, 512),
                                         ALU.mult)
                        ins = ve.tensor_copy(hbh, htmp)
                        b.inc(ins, "VE")
                        b.wait(ve, "VE")
                        ins = ve.tensor_tensor(hbl, htmp, hbh, ALU.subtract)
                        b.inc(ins, "HH")
                        up_rel[p] = b.c["HH"]
                    b.wait(te, "HH")
                    b.wait(te, "D0")
                    for tc in range(4):
                        for dg in range(4):
                            key = dg
                            if key in fl_rel:
                                b.wait(te, "FL", fl_rel[key])
                            pd = PS[:, 2048 + dg * 512:2048 + (dg + 1) * 512]
                            for il in range(IG):
                                hh_ = FBT["HBH"][:, il * 512 + tc * 128:
                                                 il * 512 + tc * 128 + 128]
                                hl_ = FBT["HBL"][:, il * 512 + tc * 128:
                                                 il * 512 + tc * 128 + 128]
                                dh_ = FBT[f"WDH{il}"][:, dg * 512:(dg + 1) * 512]
                                dl_ = FBT[f"WDL{il}"][:, dg * 512:(dg + 1) * 512]
                                te.matmul(pd, hh_, dh_, start=(il == 0), stop=False)
                                te.matmul(pd, hl_, dh_, start=False, stop=False)
                                last = te.matmul(pd, hh_, dl_, start=False,
                                                 stop=(il == IG - 1))
                            b.inc(last, "MD")
                            b.wait(ve, "MD")
                            accs = BIG[:, L["ACC"] + tc * 2048 + dg * 512:
                                       L["ACC"] + tc * 2048 + dg * 512 + 512]
                            if e == 0 and ig == 0:
                                b.wait(ve, "LX")
                                ins = ve.tensor_copy(accs, pd)
                            else:
                                ins = ve.tensor_tensor(accs, accs, pd, ALU.add)
                            b.inc(ins, "FL")
                            b.wait(ve, "FL")
                            fl_rel[key] = b.c["FL"]
                    grp_fl[g] = b.c["FL"]
                    dgrp[0] += 1
            for tc in range(4):
                t0 = q * TQ + tc * 128
                lx = bigv(L["LNX0"], 2048)
                key = 0
                if key in lnx_rel:
                    b.wait(se, "TE", lnx_rel[key])
                b.wait(se, "VE")
                b.dma(se, lx, src[t0:t0 + 128, :], "LX")
                b.wait(ve, "LX")
                b.wait(ve, "FL")
                accs = BIG[:, L["ACC"] + tc * 2048:L["ACC"] + (tc + 1) * 2048]
                ins = ve.scalar_tensor_tensor(accs, accs, 0.5, lx, ALU.mult, ALU.add)
                b.inc(ins, "VE")
                b.wait(ve, "VE")
                b.inc(ve.nop(), "VE")
                b.wait(se, "VE")
                b.dma(se, dst[t0:t0 + 128, :], accs, "LX")
            acc_store[0] = 1

    emit_ffn(0, x_in, x1s)
    b.barrier()
    if stop == "ffn1":
        b.wait(se, "LG")
        b.wait(se, "LX")
        ctx.close()
        return nc

    def build_rope_tables():
        ins = gp.iota(ioti[:, 0:16], [[128, 16]], base=0, channel_multiplier=1)
        b.inc(ins, "GE")
        ins = gp.iota(ioti[0:1, 16:80], [[1, 64]], base=0, channel_multiplier=0)
        b.inc(ins, "GE")
        b.wait(ve, "GE")
        tv = bigv(A["TMPA"], 2048)
        ve.tensor_copy(tv[:, 0:16], ioti[:, 0:16])
        ins = ve.tensor_copy(tv[0:1, 16:80], ioti[0:1, 16:80])
        b.inc(ins, "VE")
        b.wait(se, "VE")
        ins = se.activation(tv[0:1, 80:144], tv[0:1, 16:80], AF.Exp,
                            bias=0.0, scale=-math.log(10000.0) / 64.0)
        b.inc(ins, "SE")
        b.wait(te, "SE")
        pv = PS[:, 0:512]
        ins = te.matmul(pv[:, 0:64], ones1[:], tv[0:1, 80:144], start=True, stop=True)
        b.inc(ins, "TE")
        b.wait(ve, "TE")
        inv_rep = bigv(A["TMPB"], 2048)
        ins = ve.tensor_copy(inv_rep[:, 0:64], pv[:, 0:64])
        b.inc(ins, "VE")
        b.wait(ve, "VE")
        fr = bigv(A["QR"], 2048)
        for tc in range(16):
            ins = ve.tensor_scalar(fr[:, tc * 64:(tc + 1) * 64], inv_rep[:, 0:64],
                                   tv[:, tc:tc + 1], None, ALU.mult)
        b.inc(ins, "VE")
        b.wait(ve, "VE")
        ksv = bigv(A["TOK"], 2048)
        b.dma(sy, ksv[:, 0:1024], ksd[:, :], "LD")
        b.dma(sy, ksv[:, 1024:2048], kcd[:, :], "LD")
        b.wait(ve, "LD")
        sa = bigv(A["TMPA"], 2048)
        ins = ve.scalar_tensor_tensor(sa[:, 0:1024], ksv[:, 0:1024], -TWO_PI_HI,
                                      fr[:, 0:1024], ALU.mult, ALU.add)
        b.inc(ins, "VE")
        b.wait(ve, "VE")
        ins = ve.scalar_tensor_tensor(sa[:, 0:1024], ksv[:, 0:1024], -TWO_PI_LO,
                                      sa[:, 0:1024], ALU.mult, ALU.add)
        b.inc(ins, "VE")
        b.wait(ve, "VE")
        ins = ve.tensor_scalar(sa[:, 0:1024], sa[:, 0:1024], math.pi, -math.pi,
                               ALU.min, ALU.max)
        b.inc(ins, "VE")
        b.wait(se, "VE")
        ins = se.activation(sinT[:], sa[:, 0:1024], AF.Sin)
        b.inc(ins, "SE")
        ins = ve.tensor_scalar(sa[:, 1024:2048], fr[:, 0:1024], math.pi / 2, None, ALU.add)
        b.inc(ins, "VE")
        b.wait(ve, "VE")
        ins = ve.scalar_tensor_tensor(sa[:, 1024:2048], ksv[:, 1024:2048], -TWO_PI_HI,
                                      sa[:, 1024:2048], ALU.mult, ALU.add)
        b.inc(ins, "VE")
        b.wait(ve, "VE")
        ins = ve.scalar_tensor_tensor(sa[:, 1024:2048], ksv[:, 1024:2048], -TWO_PI_LO,
                                      sa[:, 1024:2048], ALU.mult, ALU.add)
        b.inc(ins, "VE")
        b.wait(ve, "VE")
        ins = ve.tensor_scalar(sa[:, 1024:2048], sa[:, 1024:2048], math.pi, -math.pi,
                               ALU.min, ALU.max)
        b.inc(ins, "VE")
        b.wait(se, "VE")
        ins = se.activation(cosT[:], sa[:, 1024:2048], AF.Sin)
        b.inc(ins, "SE")
        b.dma(se, small[0:1, 96:112], qg_in.rearrange("(o a) -> o a", o=1), "LG")
        b.wait(te, "LG")
        b.wait(te, "SE")
        ins = te.matmul(pv[:, 0:16], ones1[:], small[0:1, 96:112],
                        start=True, stop=True)
        b.inc(ins, "TE")
        b.wait(ve, "TE")
        ins = ve.tensor_copy(qgr[:], pv[:, 0:16])
        b.inc(ins, "VE")
        b.wait(ve, "VE")
        with nc.allow_non_contiguous_dma(reason="tiny one-time bias image loads"):
            b.dma(se, biasq[:], bq_in.rearrange("(a p) -> p a", p=128), "LG")
            b.dma(se, biask[:], bk_in.rearrange("(a p) -> p a", p=128), "LG")
            b.dma(se, biasv[:], bv_in.rearrange("(a p) -> p a", p=128), "LG")

    def emit_qkv():
        for q in range(NQ):
            emit_ln_quarter(x1s, q, A["XLNF"], 2048, lx_off=A["QR"])
        if "xlnfd" in dbg:
            b.wait(se, "VE")
            b.dma(se, xlnfd[:, :], BIG[:, 0:32768], "LG")
        chunks = ([("q", ec, qTw[ec], biasq[:, ec:ec + 1], ec) for ec in range(16)]
                  + [("k", ec, kTw[ec], biask[:, ec:ec + 1], 16 + ec)
                     for ec in range(4)]
                  + [("v", ec, vTw[ec], biasv[:, ec:ec + 1], 20 + ec)
                     for ec in range(4)])
        wpj_rel = {}
        for j, (kind, ec, wtile, bias_col, srow) in enumerate(chunks):
            p = j % 2
            if j >= 2:
                b.wait(sy, "TE", wpj_rel[p])
            wpj = bigv(A[f"WPJ{p}"], 2048)
            b.dma(sy, wpj, wtile, f"W{p}")
            b.wait(te, f"W{p}")
            qr = bigv(A["QR"], 2048)
            for tg in range(4):
                pv = PS[:, tg * 512:(tg + 1) * 512]
                b.wait(te, "VE")
                for dc in range(NDC):
                    last = te.matmul(pv, wpj[:, dc * 128:(dc + 1) * 128],
                                     bigv(A["XLNF"] + dc * 2048 + tg * 512, 512),
                                     start=(dc == 0), stop=(dc == NDC - 1))
                b.inc(last, "TE")
                b.wait(ve, "TE")
                ins = ve.tensor_scalar(qr[:, tg * 512:(tg + 1) * 512], pv,
                                       bias_col, None, ALU.add)
                b.inc(ins, "VE")
            wpj_rel[p] = b.c["TE"]
            if kind == "v":
                b.wait(se, "VE")
                b.dma(se, qkvs[srow], qr, "LG")
                continue
            tok = bigv(A["TOK"], 2048)
            for i0 in range(0, 16, 4):
                bank = 4 + (i0 // 4) % 2
                pv = PS[:, bank * 512:(bank + 1) * 512]
                b.wait(te, "VE")
                for qq in range(4):
                    tc = i0 + qq
                    last = te.transpose(pv[:, qq * 128:(qq + 1) * 128],
                                        qr[:, tc * 128:(tc + 1) * 128], ident[:])
                b.inc(last, "TE")
                b.wait(ve, "TE")
                ins = ve.tensor_copy(tok[:, i0 * 128:i0 * 128 + 512], pv)
                b.inc(ins, "VE")
            tmpa = bigv(A["TMPA"], 2048)
            tmpb = bigv(A["TMPB"], 2048)
            b.wait(se, "VE")
            ins = se.activation(tmpa, tok, AF.Square)
            b.inc(ins, "SE")
            b.wait(ve, "SE")
            ins = ve.tensor_reduce(small[:, 32:48], r3(tmpa, 128), AX.X, ALU.add)
            b.inc(ins, "VE")
            b.wait(ve, "VE")
            ins = ve.tensor_scalar(small[:, 32:48], small[:, 32:48],
                                   1.0 / HD, RMSEPS, ALU.mult, ALU.add)
            b.inc(ins, "VE")
            b.wait(se, "VE")
            ins = se.activation(small[:, 48:64], small[:, 32:48], AF.Sqrt)
            b.inc(ins, "SE")
            b.wait(ve, "SE")
            ins = ve.reciprocal(small[:, 64:80], small[:, 48:64])
            b.inc(ins, "VE")
            b.wait(ve, "VE")
            for tc in range(16):
                if kind == "q":
                    ins = ve.tensor_scalar(tok[:, tc * 128:(tc + 1) * 128],
                                           tok[:, tc * 128:(tc + 1) * 128],
                                           small[:, 64 + tc:65 + tc],
                                           qgr[:, ec:ec + 1], ALU.mult, ALU.mult)
                else:
                    ins = ve.tensor_scalar(tok[:, tc * 128:(tc + 1) * 128],
                                           tok[:, tc * 128:(tc + 1) * 128],
                                           small[:, 64 + tc:65 + tc], None, ALU.mult)
            b.inc(ins, "VE")
            b.wait(ve, "VE")
            top = r3(tok, 128)[:, :, 0:64]
            bot = r3(tok, 128)[:, :, 64:128]
            cosv = r3(cosT[:], 64)
            sinv = r3(sinT[:], 64)
            t1 = r3(tmpa, 64)[:, 0:16, :]
            t2 = r3(tmpa, 64)[:, 16:32, :]
            t3 = r3(tmpb, 64)[:, 0:16, :]
            t4 = r3(tmpb, 64)[:, 16:32, :]
            ve.tensor_tensor(t1, top, cosv, ALU.mult)
            ve.tensor_tensor(t2, bot, sinv, ALU.mult)
            ve.tensor_tensor(t3, top, sinv, ALU.mult)
            ve.tensor_tensor(t4, bot, cosv, ALU.mult)
            ve.tensor_tensor(top, t1, t2, ALU.add)
            ins = ve.tensor_tensor(bot, t4, t3, ALU.subtract)
            b.inc(ins, "VE")
            b.wait(te, "VE")
            for i0 in range(0, 16, 4):
                bank = 4 + (i0 // 4) % 2
                pv = PS[:, bank * 512:(bank + 1) * 512]
                b.wait(te, "VE")
                for qq in range(4):
                    tc = i0 + qq
                    last = te.transpose(pv[:, qq * 128:(qq + 1) * 128],
                                        tok[:, tc * 128:(tc + 1) * 128], ident[:])
                b.inc(last, "TE")
                b.wait(ve, "TE")
                ins = ve.tensor_copy(qr[:, i0 * 128:i0 * 128 + 512], pv)
                b.inc(ins, "VE")
            b.wait(se, "VE")
            b.dma(se, qkvs[srow], qr, "LG")

    def emit_masks():
        mk = bigv(C["MASKS"], 2048)
        for tqc in range(4):
            b.wait(gp, "VE")
            ins = gp.iota(ioti[:], [[1, 512]], base=-tqc * 128, channel_multiplier=-1)
            b.inc(ins, "GE")
            b.wait(ve, "GE")
            mv = mk[:, tqc * 512:(tqc + 1) * 512]
            ve.tensor_copy(mv, ioti[:])
            ins = ve.tensor_scalar(mv, mv, 0.0, NEG, ALU.is_gt, ALU.mult)
            b.inc(ins, "VE")
            b.wait(ve, "VE")

    def emit_attn_core():
        mk = bigv(C["MASKS"], 2048)
        sc_rel = [0]
        av_rel = [0]
        b.wait(sy, "LG")
        for g in range(NKV):
            kt = bigv(C["KT"], 2048)
            vt = bigv(C["VT"], 2048)
            vg = bigv(C["VG"], 2048)
            b.wait(sy, "TE")
            b.dma(sy, kt, qkvs[16 + g], "LA")
            b.dma(sy, vt, qkvs[20 + g], "LA")
            b.wait(te, "LA")
            for i0 in range(0, 16, 4):
                pv = PS[:, 3584:4096]
                b.wait(te, "VE")
                for qq in range(4):
                    kc = i0 + qq
                    last = te.transpose(pv[:, qq * 128:(qq + 1) * 128],
                                        vt[:, kc * 128:(kc + 1) * 128], ident[:])
                b.inc(last, "TE")
                b.wait(ve, "TE")
                ins = ve.tensor_copy(vg[:, i0 * 128:i0 * 128 + 512], pv)
                b.inc(ins, "VE")
            for qh in range(4):
                h = g * 4 + qh
                qt = bigv(C["QT"], 2048)
                b.wait(sy, "TE")
                b.dma(sy, qt, qkvs[h], "LB")
                b.wait(te, "LB")
                aot = bigv(C["AOT"], 2048)
                for tqg in range(4):
                    nb = tqg + 1
                    for tqc in range(4):
                        tq0 = tqg * 512 + tqc * 128
                        b.wait(te, "VE", sc_rel[0])
                        for kb in range(nb):
                            pv = PS[:, kb * 512:(kb + 1) * 512]
                            ins = te.matmul(pv, qt[:, tq0:tq0 + 128],
                                            kt[:, kb * 512:(kb + 1) * 512],
                                            start=True, stop=True)
                        b.inc(ins, "TE")
                        ss = bigv(C["SS"], 2048)
                        pp = bigv(C["PP"], 2048)
                        b.wait(ve, "TE")
                        ins = ve.tensor_copy(ss[:, 0:nb * 512], PS[:, 0:nb * 512])
                        b.inc(ins, "VE")
                        sc_rel[0] = b.c["VE"]
                        dia = ss[:, (nb - 1) * 512: nb * 512]
                        ve.tensor_tensor(dia, dia, mk[:, tqc * 512:(tqc + 1) * 512],
                                         ALU.add)
                        ins = ve.reduce_max(small[:, 0:1], ss[:, 0:nb * 512], AX.X)
                        b.inc(ins, "VE")
                        b.wait(ve, "VE")
                        ins = ve.tensor_scalar(small[:, 1:2], small[:, 0:1], -CSC,
                                               None, ALU.mult)
                        b.inc(ins, "VE")
                        b.wait(se, "VE")
                        ins = se.activation(pp[:, 0:nb * 512], ss[:, 0:nb * 512],
                                            AF.Exp, bias=small[:, 1:2], scale=CSC)
                        b.inc(ins, "SE")
                        b.wait(ve, "SE")
                        ins = ve.reduce_sum(small[:, 2:3], pp[:, 0:nb * 512], AX.X)
                        b.inc(ins, "VE")
                        b.wait(ve, "VE")
                        ins = ve.reciprocal(small[:, 3:4], small[:, 2:3])
                        b.inc(ins, "VE")
                        b.wait(ve, "VE")
                        ins = ve.tensor_scalar(pp[:, 0:nb * 512], pp[:, 0:nb * 512],
                                               small[:, 3:4], None, ALU.mult)
                        b.inc(ins, "VE")
                        b.wait(te, "VE")
                        for k0 in range(0, nb * 4, 4):
                            bank = 4 + (k0 // 4) % 2
                            pv = PS[:, bank * 512:(bank + 1) * 512]
                            b.wait(te, "VE")
                            for qq in range(4):
                                kc = k0 + qq
                                last = te.transpose(pv[:, qq * 128:(qq + 1) * 128],
                                                    pp[:, kc * 128:(kc + 1) * 128],
                                                    ident[:])
                            b.inc(last, "TE")
                            b.wait(ve, "TE")
                            dstv = r3(BIG[:, C["PT"] + k0 * 512:
                                          C["PT"] + (k0 + 4) * 512], 512)[
                                :, :, tqc * 128: tqc * 128 + 128]
                            ins = ve.tensor_copy(dstv, r3(pv, 128))
                            b.inc(ins, "VE")
                    pv = PS[:, 3072:3584]
                    b.wait(te, "VE")
                    b.wait(te, "VE", av_rel[0])
                    for kc in range(nb * 4):
                        last = te.matmul(pv, vg[:, kc * 128:(kc + 1) * 128],
                                         bigv(C["PT"] + kc * 512, 512),
                                         start=(kc == 0), stop=(kc == nb * 4 - 1))
                    b.inc(last, "TE")
                    b.wait(ve, "TE")
                    ins = ve.tensor_copy(aot[:, tqg * 512:(tqg + 1) * 512], pv)
                    b.inc(ins, "VE")
                    av_rel[0] = b.c["VE"]
                b.wait(se, "VE")
                b.dma(se, aos[h], aot, "LG")

    def emit_oproj():
        borow = bigv(O["XR0"], 2048)
        b.wait(se, "LG")
        b.wait(se, "LX")
        b.dma(se, borow[0:1, :], bo_in.rearrange("(o a) -> o a", o=1), "LX")
        b.wait(te, "LX")
        b.wait(te, "VE")
        for sgl in range(4):
            pv = PS[:, sgl * 512:(sgl + 1) * 512]
            ins = te.matmul(pv, ones1[:], borow[0:1, sgl * 512:(sgl + 1) * 512],
                            start=True, stop=True)
        b.inc(ins, "TE")
        b.wait(ve, "TE")
        ins = ve.tensor_copy(bigv(O["BOREP"], 2048), PS[:, 0:2048])
        b.inc(ins, "VE")
        b.wait(sy, "LG")
        for ec in range(16):
            b.dma(sy, bigv(O["AOTF"] + ec * 2048, 2048), aos[ec], "LA")
        aotf_ld = b.c["LA"]
        wot_rel = {}
        for tcg in range(8):
            for ec in range(16):
                p = ec % 2
                if p in wot_rel:
                    b.wait(sy, "TE", wot_rel[p])
                wot = bigv(O[f"WOT{p}"], 2048)
                b.dma(sy, wot, oTw[ec], f"W{p}")
                b.wait(te, f"W{p}")
                if ec == 0:
                    b.wait(te, "LA", aotf_ld)
                    b.wait(te, "VE")
                for tc2 in range(2):
                    tc = tcg * 2 + tc2
                    for dg in range(4):
                        pv = PS[:, tc2 * 2048 + dg * 512: tc2 * 2048 + (dg + 1) * 512]
                        last = te.matmul(
                            pv,
                            bigv(O["AOTF"] + ec * 2048 + tc * 128, 128),
                            wot[:, dg * 512:(dg + 1) * 512],
                            start=(ec == 0), stop=(ec == 15))
                b.inc(last, "TE")
                wot_rel[p] = b.c["TE"]
            for tc2 in range(2):
                tc = tcg * 2 + tc2
                t0 = tc * 128
                xr = bigv(O[f"XR{tc2}"], 2048)
                b.wait(se, "VE")
                b.dma(se, xr, x1s[t0:t0 + 128, :], "LX")
                b.wait(ve, "TE")
                b.wait(ve, "LX")
                pv = PS[:, tc2 * 2048:(tc2 + 1) * 2048]
                ve.tensor_tensor(xr, xr, pv, ALU.add)
                ins = ve.tensor_tensor(xr, xr, bigv(O["BOREP"], 2048), ALU.add)
                b.inc(ins, "VE")
                b.wait(se, "VE")
                b.dma(se, x2s[t0:t0 + 128, :], xr, "LX")

    build_rope_tables()
    emit_qkv()
    b.barrier()
    emit_masks()
    emit_attn_core()
    b.barrier()
    emit_oproj()
    b.barrier()
    if stop == "attn":
        b.wait(se, "LG")
        b.wait(se, "LX")
        ctx.close()
        return nc

    emit_ffn(1, x2s, y_out)
    b.wait(se, "LG")
    b.wait(se, "LX")
    ctx.close()
    return nc


def prep_inputs(inputs):
    x = np.ascontiguousarray(np.asarray(inputs["L0_attn_x"], dtype=np.float32))
    w = {
        "wq": inputs["L0_attn_Wq"], "bq": inputs["L0_attn_bq"],
        "wk": inputs["L0_attn_Wk"], "bk": inputs["L0_attn_bk"],
        "wv": inputs["L0_attn_Wv"], "bv": inputs["L0_attn_bv"],
        "wo": inputs["L0_attn_Wo"], "bo": inputs["L0_attn_bo"],
        "qg": inputs["L0_attn_q_gain"], "rw": inputs["L0_ffn_router_w"],
        "gw": inputs["L0_ffn_gate_w"], "uw": inputs["L0_ffn_up_w"],
        "dw": inputs["L0_ffn_down_w"],
    }
    w = {k: np.ascontiguousarray(np.asarray(v, dtype=np.float32))
         for k, v in w.items()}
    return x, w


def kernel(**inputs):
    x, w = prep_inputs(inputs)
    nc = build_nc(stop="full")
    in_maps = [dict(x=x[i], **w) for i in range(8)]
    res = run_bass_kernel_spmd(nc, in_maps, core_ids=list(range(8)))
    return np.stack([r["y"] for r in res.results]).astype(np.float32)


# version 04
# speedup vs baseline: 1.1362x; 1.0140x over previous
import math
from contextlib import ExitStack

import numpy as np
import concourse.bass as bass
import concourse.mybir as mybir
from concourse.bass_utils import run_bass_kernel_spmd

F32 = mybir.dt.float32
BF16 = mybir.dt.bfloat16
I32 = mybir.dt.int32
AF = mybir.ActivationFunctionType
ALU = mybir.AluOpType
AX = mybir.AxisListType

S = 2048
Dm = 2048
IFF = 4096
NE = 8
NH = 16
NKV = 4
HD = 128
NQ = 4
TQ = 512
IG = 4
NIC = IFF // 128
NDC = Dm // 128
LNEPS = 1e-5
RMSEPS = 1.1920929e-07
TOPP = 0.9
CSC = 1.0 / math.sqrt(HD)
NEG = -1e30
TWO_PI_HI = float(np.float32(2 * math.pi))
TWO_PI_LO = float(np.float64(2 * math.pi) - np.float64(np.float32(2 * math.pi)))


def rope_host_tables():
    t = (np.arange(16)[None, :, None] * 128 + np.arange(128)[:, None, None])
    j = np.arange(64)[None, None, :]
    inv = 1.0 / 10000.0 ** (j / 64.0)
    fr = (t * inv).astype(np.float64)
    ks = np.round(fr / (2 * np.pi))
    kc = np.round((fr + np.pi / 2) / (2 * np.pi))
    return (ks.reshape(128, 1024).astype(np.float32),
            kc.reshape(128, 1024).astype(np.float32))


def r3(ap, blk):
    return ap.rearrange("p (a b) -> p a b", b=blk)


def build_nc(stop="full", dbg=()):
    nc = bass.Bass("TRN2", target_bir_lowering=False, debug=False)
    ctx = ExitStack()

    x_in = nc.dram_tensor("x", [S, Dm], F32, kind="ExternalInput").ap()
    wq_in = nc.dram_tensor("wq", [Dm, Dm], F32, kind="ExternalInput").ap()
    bq_in = nc.dram_tensor("bq", [Dm], F32, kind="ExternalInput").ap()
    wk_in = nc.dram_tensor("wk", [512, Dm], F32, kind="ExternalInput").ap()
    bk_in = nc.dram_tensor("bk", [512], F32, kind="ExternalInput").ap()
    wv_in = nc.dram_tensor("wv", [512, Dm], F32, kind="ExternalInput").ap()
    bv_in = nc.dram_tensor("bv", [512], F32, kind="ExternalInput").ap()
    wo_in = nc.dram_tensor("wo", [Dm, Dm], F32, kind="ExternalInput").ap()
    bo_in = nc.dram_tensor("bo", [Dm], F32, kind="ExternalInput").ap()
    qg_in = nc.dram_tensor("qg", [NH], F32, kind="ExternalInput").ap()
    rw_in = nc.dram_tensor("rw", [NE, Dm], F32, kind="ExternalInput").ap()
    gw_in = nc.dram_tensor("gw", [NE, IFF, Dm], F32, kind="ExternalInput").ap()
    uw_in = nc.dram_tensor("uw", [NE, IFF, Dm], F32, kind="ExternalInput").ap()
    dw_in = nc.dram_tensor("dw", [NE, Dm, IFF], F32, kind="ExternalInput").ap()
    y_out = nc.dram_tensor("y", [S, Dm], F32, kind="ExternalOutput").ap()

    def scrt(name, shape):
        kind = "ExternalOutput" if name in dbg else "Internal"
        return nc.dram_tensor(name, shape, F32, kind=kind).ap()

    def scrtb(name, shape):
        kind = "ExternalOutput" if name in dbg else "Internal"
        return nc.dram_tensor(name, shape, BF16, kind=kind).ap()
    gTh = scrtb("gTh", [NE, NIC, 128, Dm])
    gTl = scrtb("gTl", [NE, NIC, 128, Dm])
    uTh = scrtb("uTh", [NE, NIC, 128, Dm])
    uTl = scrtb("uTl", [NE, NIC, 128, Dm])
    dTh = scrtb("dTh", [NE, NIC, 128, Dm])
    dTl = scrtb("dTl", [NE, NIC, 128, Dm])
    qTw = scrt("qTw", [16, 128, Dm])
    kTw = scrt("kTw", [4, 128, Dm])
    vTw = scrt("vTw", [4, 128, Dm])
    oTw = scrt("oTw", [16, 128, Dm])
    rTw = scrt("rTw", [128, 128])
    xlnfd = scrt("xlnfd", [128, 32768])
    x1s = scrt("x1s", [S, Dm])
    x2s = scrt("x2s", [S, Dm])
    qkvs = scrt("qkvs", [24, 128, S])
    aos = scrt("aos", [16, 128, S])

    ks_np, kc_np = rope_host_tables()
    ksd = nc.inline_tensor(ks_np, name="ks_d").ap()
    kcd = nc.inline_tensor(kc_np, name="kc_d").ap()
    idn = nc.inline_tensor(np.eye(128, dtype=np.float32), name="idn_d").ap()

    ident = nc.alloc_sbuf_tensor("ident", [128, 128], F32).ap()
    ones1 = nc.alloc_sbuf_tensor("ones1", [1, 128], F32).ap()
    qgr = nc.alloc_sbuf_tensor("qgr", [128, 16], F32).ap()
    biasq = nc.alloc_sbuf_tensor("biasq", [128, 16], F32).ap()
    biask = nc.alloc_sbuf_tensor("biask", [128, 4], F32).ap()
    biasv = nc.alloc_sbuf_tensor("biasv", [128, 4], F32).ap()
    stats = nc.alloc_sbuf_tensor("stats", [128, 32], F32).ap()
    small = nc.alloc_sbuf_tensor("small", [128, 128], F32).ap()
    ioti = nc.alloc_sbuf_tensor("ioti", [128, 512], I32).ap()
    cosT = nc.alloc_sbuf_tensor("cosT", [128, 1024], F32).ap()
    sinT = nc.alloc_sbuf_tensor("sinT", [128, 1024], F32).ap()
    rt_sb = nc.alloc_sbuf_tensor("rt_sb", [128, 128], F32).ap()
    rthl = nc.alloc_sbuf_tensor("rthl", [128, 256], BF16).ap()
    BIGC = 45184
    BIGt = nc.alloc_sbuf_tensor("BIG", [128, BIGC], F32)
    BIG = BIGt.ap()
    big_addr = nc.lookup_mloc(BIGt).addr
    BFOFF = 16512
    BFC = (BIGC - BFOFF) * 2
    PSg = ctx.enter_context(nc.psum_tensor("PSb", [128, 4096], F32))
    PS = PSg

    def bigv(off, n):
        return BIG[:, off:off + n]



    def mklay(*items):
        d, off = {}, 0
        for name, n in items:
            d[name] = off
            off += n
        assert off <= BIGC, (off, items[0])
        return d

    L = mklay(("ACC", 8192), ("WREPA", 4096), ("LNX0", 2048),
              ("GSB0", 512), ("GSB1", 512), ("HTMP", 512),
              ("WROW", 512), ("WALL", 32))
    assert L["WALL"] + 32 <= BFOFF
    FBT = {}
    off2 = 0
    for name, n in [("XLH", 8192), ("XLL", 8192),
                    ("WGH0", 2048), ("WGH1", 2048), ("WGL0", 2048), ("WGL1", 2048),
                    ("WUH0", 2048), ("WUH1", 2048), ("WUL0", 2048), ("WUL1", 2048),
                    *[(f"WDH{k}", 2048) for k in range(IG)],
                    *[(f"WDL{k}", 2048) for k in range(IG)],
                    ("HBH", 2048), ("HBL", 2048)]:
        FBT[name] = nc.alloc_sbuf_tensor_at(
            "bf_" + name, [128, n], BF16,
            offset=big_addr + BFOFF * 4 + off2 * 2).ap()
        off2 += n
    assert off2 <= BFC, (off2, BFC)
    P0_BF_MIN = (32768 - BFOFF) * 2
    for k, coff in enumerate([32768, 34816, 36864, 38912]):
        assert coff >= P0_BF_MIN
        nm = ("CONVH0", "CONVL0", "CONVH1", "CONVL1")[k]
        FBT[nm] = nc.alloc_sbuf_tensor_at(
            "bf_" + nm, [128, 2048], BF16,
            offset=big_addr + BFOFF * 4 + coff * 2).ap()
    A = mklay(("XLNF", 32768), ("WPJ0", 2048), ("WPJ1", 2048),
              ("QR", 2048), ("TOK", 2048), ("TMPA", 2048), ("TMPB", 2048))
    C = mklay(("KT", 2048), ("VT", 2048), ("VG", 2048), ("QT", 2048),
              ("SS", 2048), ("PP", 2048), ("PT", 8192), ("AOT", 2048),
              ("MASKS", 2048))
    O = mklay(("AOTF", 32768), ("WOT0", 2048), ("WOT1", 2048),
              ("BOREP", 2048), ("XR0", 2048), ("XR1", 2048))
    P0 = mklay(("NAT0", 8192), ("NAT1", 8192),
               *[(f"ASM{k}", 2048) for k in range(8)])

    class Bld:
        def __init__(self):
            self.sm, self.c = {}, {}
            for nm in ["LD", "LG", "TE", "VE", "SE", "GE", "BR",
                       "DF", "SIL", "HH", "FL", "MG", "MU", "MD",
                       "W0", "W1", "D0", "LA", "LB", "LX", "G0", "G1",
                       "N0", "N1"]:
                self.sm[nm] = ctx.enter_context(nc.semaphore("s" + nm))
                self.c[nm] = 0

        def inc(self, ins, s, n=1):
            ins.then_inc(self.sm[s], n)
            self.c[s] += n
            return self.c[s]

        def dma(self, eng, out, in_, s):
            return self.inc(eng.dma_start(out=out, in_=in_), s, 16)

        def wait(self, eng, s, v=None):
            v = self.c[s] if v is None else v
            if v > 0:
                eng.wait_ge(self.sm[s], v)

        def barrier(self):
            engs = [nc.sync, nc.tensor, nc.vector, nc.scalar, nc.gpsimd]
            for e in engs:
                self.inc(e.nop(), "BR", 1)
            v = self.c["BR"]
            for e in engs:
                e.wait_ge(self.sm["BR"], v)

    b = Bld()
    sy, te, ve, se, gp = nc.sync, nc.tensor, nc.vector, nc.scalar, nc.gpsimd

    b.dma(sy, ident[:], idn[:], "LD")
    b.wait(ve, "LD")
    ins = ve.memset(ones1[:], 1.0)
    b.inc(ins, "VE")

    nat_rel = []
    slab_i = [0]
    asm_free = {}
    conv_free = {}
    conv_cnt = [0]
    p0_bank = {}

    def do_slab(dmf, nsub, ninner, col_fn, outs, tile_cols, nat_part=128,
                bf_outs=None):
        si = slab_i[0]
        slab_i[0] += 1
        nsem = f"N{si % 2}"
        natv = bigv(P0[f"NAT{si % 2}"], 8192)
        if si >= 2:
            b.wait(sy, "TE", nat_rel[si - 2])
        dmf(natv, sem=nsem)
        b.wait(te, nsem)
        for j in range(nsub):
            slot = (si % 2) * 4 + j
            av = bigv(P0[f"ASM{slot}"], 2048)
            if slot in asm_free:
                sm_, vv_ = asm_free[slot]
                b.wait(se, sm_, vv_)
            for i0 in range(0, ninner, 4):
                bank = (i0 // 4) % 4
                pv = PS[:, bank * 512:(bank + 1) * 512]
                b.wait(te, "SE", p0_bank.get(bank, 0))
                last = None
                for q in range(4):
                    i = i0 + q
                    if i >= ninner:
                        break
                    src = natv[0:nat_part, col_fn(j, i)]
                    w = src.shape[-1]
                    last = te.transpose(pv[0:w, q * nat_part:q * nat_part + nat_part],
                                        src, ident[:])
                b.inc(last, "TE")
                b.wait(se, "TE")
                n = min(4, ninner - i0) * nat_part
                ins = se.copy(av[:, i0 * nat_part:i0 * nat_part + n], pv[:, 0:n])
                b.inc(ins, "SE")
                p0_bank[bank] = b.c["SE"]
            b.wait(se, "SE")
            b.inc(se.nop(), "SE")
            if bf_outs is None:
                b.dma(se, outs[j], av[:, 0:tile_cols], "LG")
                asm_free[slot] = ("LG", b.c["LG"])
            else:
                p = conv_cnt[0] % 2
                use_gp = False
                conv_cnt[0] += 1
                ch = FBT[f"CONVH{p}"]
                cl = FBT[f"CONVL{p}"]
                if p in conv_free:
                    b.wait(ve, f"G{p}")
                b.wait(ve, "SE")
                ins = ve.tensor_copy(ch, av[:, 0:2048])
                b.inc(ins, "VE")
                b.wait(ve, "VE")
                if use_gp:
                    b.wait(gp, "VE")
                    ins = gp.tensor_tensor(cl, av[:, 0:2048], ch, ALU.subtract)
                    b.inc(ins, "GE")
                    b.wait(se, "GE")
                    b.wait(se, "VE")
                else:
                    ins = ve.tensor_tensor(cl, av[:, 0:2048], ch, ALU.subtract)
                    b.inc(ins, "VE")
                    b.wait(ve, "VE")
                    b.inc(ve.nop(), "VE")
                    b.wait(se, "VE")
                hi_t, lo_t = bf_outs[j]
                b.dma(se, hi_t, ch, f"G{p}")
                b.dma(se, lo_t, cl, f"G{p}")
                conv_free[p] = 1
                asm_free[slot] = ("VE", b.c["VE"]) if not use_gp else ("GE", b.c["GE"])
        nat_rel.append(b.c["TE"])

    def image_weight(w_nat, O_rows, tiles_out, bf_tiles=None):
        for s_i in range(O_rows // 512):
            def dmf(natv, s_i=s_i, sem="N0"):
                for j in range(4):
                    b.dma(sy, natv[:, j * 2048:(j + 1) * 2048],
                          w_nat[s_i * 512 + j * 128: s_i * 512 + (j + 1) * 128, :],
                          sem)
            do_slab(dmf, 4, 16,
                    lambda j, i: slice(j * 2048 + i * 128, j * 2048 + (i + 1) * 128),
                    [tiles_out[s_i * 4 + j] for j in range(4)] if tiles_out else None,
                    2048,
                    bf_outs=None if bf_tiles is None else
                    [bf_tiles[s_i * 4 + j] for j in range(4)])

    def plain_weight(w_nat, I_cols, tiles_out, bf_tiles=None):
        for s_i in range(I_cols // 512):
            def dmf(natv, s_i=s_i, sem="N0"):
                src = w_nat[:, s_i * 512:(s_i + 1) * 512].rearrange(
                    "(a p) c -> p a c", p=128)
                b.dma(sy, r3(natv, 512), src, sem)
            do_slab(dmf, 4, 16,
                    lambda j, i: slice(i * 512 + j * 128, i * 512 + j * 128 + 128),
                    [tiles_out[s_i * 4 + j] for j in range(4)] if tiles_out else None,
                    2048,
                    bf_outs=None if bf_tiles is None else
                    [bf_tiles[s_i * 4 + j] for j in range(4)])

    for e in range(NE):
        image_weight(gw_in[e], IFF, None,
                     bf_tiles=[(gTh[e, ic], gTl[e, ic]) for ic in range(NIC)])
        image_weight(uw_in[e], IFF, None,
                     bf_tiles=[(uTh[e, ic], uTl[e, ic]) for ic in range(NIC)])
        plain_weight(dw_in[e], IFF, None,
                     bf_tiles=[(dTh[e, ic], dTl[e, ic]) for ic in range(NIC)])
    image_weight(wq_in, Dm, [qTw[i] for i in range(16)])
    image_weight(wk_in, 512, [kTw[i] for i in range(4)])
    image_weight(wv_in, 512, [vTw[i] for i in range(4)])
    plain_weight(wo_in, Dm, [oTw[i] for i in range(16)])
    si = slab_i[0]
    slab_i[0] += 1
    natv = bigv(P0[f"NAT{si % 2}"], 8192)
    if si >= 2:
        b.wait(sy, "TE", nat_rel[si - 2])
    b.dma(sy, natv[0:8, 0:2048], rw_in[:, :], f"N{si % 2}")
    b.wait(te, f"N{si % 2}")
    av = bigv(P0[f"ASM{(si % 2) * 4}"], 2048)
    for i0 in range(0, 16, 4):
        bank = (i0 // 4) % 2
        pv = PS[:, bank * 512:(bank + 1) * 512]
        b.wait(te, "VE")
        b.wait(te, "SE")
        for q in range(4):
            dc = i0 + q
            last = te.transpose(pv[:, q * 8:(q + 1) * 8],
                                natv[0:8, dc * 128:(dc + 1) * 128], ident[0:8, 0:8])
        b.inc(last, "TE")
        b.wait(ve, "TE")
        ins = ve.tensor_copy(av[:, i0 * 8:i0 * 8 + 32], pv[:, 0:32])
        b.inc(ins, "VE")
    b.wait(se, "VE")
    b.dma(se, rTw[:, :], av[:, 0:128], "LG")
    nat_rel.append(b.c["TE"])

    b.barrier()
    if stop == "p0":
        b.wait(se, "LG")
        b.wait(se, "G0")
        b.wait(se, "G1")
        ctx.close()
        return nc

    lnx_rel = {}
    acc_store = [0]
    gate_rel = {}
    up_rel = {}
    wbuf_rel = {}
    dgrp = [0]
    grp_fl = {}
    fl_rel = {}
    gi_ctr = [0]

    b.wait(sy, "G0")
    b.wait(sy, "G1")
    b.wait(sy, "LG")
    b.dma(sy, rt_sb[:], rTw[:, :], "LD")
    rt_ld = b.c["LD"]
    b.wait(ve, "LD", rt_ld)
    ins = ve.tensor_copy(rthl[:, 0:128], rt_sb[:])
    b.inc(ins, "VE")
    b.wait(ve, "VE")
    ins = ve.tensor_tensor(rthl[:, 128:256], rt_sb[:], rthl[:, 0:128], ALU.subtract)
    b.inc(ins, "VE")
    b.wait(ve, "VE")
    rt_split = b.c["VE"]

    def emit_ln_quarter(src, q, dest_off, dest_blk, split=False, lx_off=None):
        for tc in range(4):
            t0 = q * TQ + tc * 128
            lx = bigv(L["LNX0"] if lx_off is None else lx_off, 2048)
            key = 0
            if key in lnx_rel:
                b.wait(se, "TE", lnx_rel[key])
            b.wait(se, "VE")
            b.dma(se, lx, src[t0:t0 + 128, :], "LX")
            b.wait(ve, "LX")
            for i in range(4):
                ins = ve.bn_stats(stats[:, i * 6:(i + 1) * 6], lx[:, i * 512:(i + 1) * 512])
            b.inc(ins, "VE")
            b.wait(ve, "VE")
            ins = ve.bn_aggr(stats[:, 24:26], stats[:, 0:24])
            b.inc(ins, "VE")
            b.wait(ve, "VE")
            ins = ve.tensor_scalar(small[:, 0:1], stats[:, 25:26], LNEPS, None, ALU.add)
            b.inc(ins, "VE")
            b.wait(se, "VE")
            ins = se.activation(small[:, 1:2], small[:, 0:1], AF.Sqrt)
            b.inc(ins, "SE")
            b.wait(ve, "SE")
            ins = ve.reciprocal(small[:, 2:3], small[:, 1:2])
            b.inc(ins, "VE")
            b.wait(ve, "VE")
            ins = ve.tensor_scalar(lx, lx, stats[:, 24:25], small[:, 2:3],
                                   ALU.subtract, ALU.mult)
            b.inc(ins, "VE")
            b.wait(te, "VE")
            for i0 in range(0, 16, 4):
                bank = 2 + (i0 // 4) % 2
                pv = PS[:, bank * 512:(bank + 1) * 512]
                if i0 == 0:
                    b.wait(te, "VE")
                else:
                    b.wait(te, "VE", b.c["VE"] - (2 if split else 1))
                for qq in range(4):
                    dc = i0 + qq
                    last = te.transpose(pv[:, qq * 128:(qq + 1) * 128],
                                        lx[:, dc * 128:(dc + 1) * 128], ident[:])
                b.inc(last, "TE")
                b.wait(ve, "TE")
                if not split:
                    toff = (q * TQ if dest_blk == 2048 else 0) + tc * 128
                    dst = r3(BIG[:, dest_off + i0 * dest_blk:
                                 dest_off + (i0 + 4) * dest_blk], dest_blk)[
                        :, :, toff: toff + 128]
                    ins = ve.tensor_copy(dst, r3(pv, 128))
                    b.inc(ins, "VE")
                else:
                    toff = tc * 128
                    dsth = r3(FBT["XLH"][:, i0 * 512:(i0 + 4) * 512], 512)[
                        :, :, toff: toff + 128]
                    dstl = r3(FBT["XLL"][:, i0 * 512:(i0 + 4) * 512], 512)[
                        :, :, toff: toff + 128]
                    ins = ve.tensor_copy(dsth, r3(pv, 128))
                    b.inc(ins, "VE")
                    b.wait(ve, "VE")
                    ins = ve.tensor_tensor(dstl, r3(pv, 128), dsth, ALU.subtract)
                    b.inc(ins, "VE")
                    b.wait(ve, "VE")
            lnx_rel[key] = b.c["TE"]

    def emit_router(q):
        wall = bigv(L["WALL"], 32)
        for tc in range(4):
            pv = PS[:, 2048:2056]
            b.wait(te, "VE")
            b.wait(te, "VE", rt_split)
            b.wait(te, "FL")
            for dc in range(NDC):
                xh = FBT["XLH"][:, dc * 512 + tc * 128: dc * 512 + tc * 128 + 128]
                xl = FBT["XLL"][:, dc * 512 + tc * 128: dc * 512 + tc * 128 + 128]
                rh = rthl[:, dc * 8:dc * 8 + 8]
                rl = rthl[:, 128 + dc * 8:128 + dc * 8 + 8]
                te.matmul(pv, xh, rh, start=(dc == 0), stop=False)
                te.matmul(pv, xl, rh, start=False, stop=False)
                last = te.matmul(pv, xh, rl, start=False,
                                 stop=(dc == NDC - 1))
            b.inc(last, "TE")
            b.wait(ve, "TE")
            pr = small[:, 8:16]
            ins = ve.tensor_copy(small[:, 16:24], pv[:, 0:8])
            b.inc(ins, "VE")
            b.wait(ve, "VE")
            ins = ve.reduce_max(small[:, 0:1], small[:, 16:24], AX.X)
            b.inc(ins, "VE")
            b.wait(ve, "VE")
            ins = ve.tensor_scalar(small[:, 1:2], small[:, 0:1], -1.0, None, ALU.mult)
            b.inc(ins, "VE")
            b.wait(se, "VE")
            ins = se.activation(pr, small[:, 16:24], AF.Exp, bias=small[:, 1:2],
                                scale=1.0)
            b.inc(ins, "SE")
            b.wait(ve, "SE")
            ins = ve.reduce_sum(small[:, 2:3], pr, AX.X)
            b.inc(ins, "VE")
            b.wait(ve, "VE")
            ins = ve.reciprocal(small[:, 3:4], small[:, 2:3])
            b.inc(ins, "VE")
            b.wait(ve, "VE")
            ins = ve.tensor_scalar(pr, pr, small[:, 3:4], None, ALU.mult)
            b.inc(ins, "VE")
            b.wait(ve, "VE")
            for e in range(NE):
                ins = ve.scalar_tensor_tensor(small[:, 24:32], pr, pr[:, e:e + 1], pr,
                                              ALU.is_gt, ALU.mult)
                b.inc(ins, "VE")
                b.wait(ve, "VE")
                ins = ve.reduce_sum(small[:, 4:5], small[:, 24:32], AX.X)
                b.inc(ins, "VE")
                b.wait(ve, "VE")
                ins = ve.tensor_scalar(small[:, 5:6], small[:, 4:5], TOPP, None, ALU.is_le)
                b.inc(ins, "VE")
                b.wait(ve, "VE")
                ins = ve.tensor_tensor(wall[:, tc * 8 + e: tc * 8 + e + 1],
                                       small[:, 5:6], pr[:, e:e + 1], ALU.mult)
            b.inc(ins, "VE")

    def emit_wrep_all(q):
        wall = bigv(L["WALL"], 32)
        wrow = bigv(L["WROW"], 512)
        for e in range(NE):
            pv = PS[:, 3584:4096]
            b.wait(te, "VE")
            b.wait(te, "FL")
            for tc in range(4):
                last = te.transpose(pv[0:1, tc * 128:(tc + 1) * 128],
                                    wall[:, tc * 8 + e: tc * 8 + e + 1], ident[:])
            b.inc(last, "TE")
            b.wait(ve, "TE")
            ins = ve.tensor_copy(wrow[0:1, :], pv[0:1, :])
            b.inc(ins, "VE")
            b.wait(te, "VE")
            ins = te.matmul(pv, ones1[:], wrow[0:1, :], start=True, stop=True)
            b.inc(ins, "TE")
            b.wait(ve, "TE")
            ins = ve.tensor_copy(bigv(L["WREPA"] + e * 512, 512), pv)
            b.inc(ins, "VE")
        b.wait(te, "VE")

    def emit_ffn(call, src, dst):
        for q in range(NQ):
            b.wait(ve, "MU")
            b.wait(ve, "TE")
            emit_ln_quarter(src, q, 0, 512, split=True)
            emit_router(q)
            emit_wrep_all(q)
            for e in range(NE):
                for ig in range(NIC // IG):
                    g = dgrp[0]
                    for il in range(IG):
                        i = ig * IG + il
                        gi = gi_ctr[0]
                        gi_ctr[0] += 1
                        p = gi % 2
                        if gi >= 2:
                            b.wait(sy, "MU", wbuf_rel.get(p, 0))
                        wgh = FBT[f"WGH{p}"]
                        wgl = FBT[f"WGL{p}"]
                        wuh = FBT[f"WUH{p}"]
                        wul = FBT[f"WUL{p}"]
                        b.dma(sy, wgh, gTh[e, i], f"W{p}")
                        b.dma(sy, wgl, gTl[e, i], f"W{p}")
                        b.dma(sy, wuh, uTh[e, i], f"W{p}")
                        b.dma(sy, wul, uTl[e, i], f"W{p}")
                        wgu_done = b.c[f"W{p}"]
                        if g >= 1:
                            b.wait(sy, "FL", grp_fl[g - 1])
                        b.dma(sy, FBT[f"WDH{il}"], dTh[e, i], "D0")
                        b.dma(sy, FBT[f"WDL{il}"], dTl[e, i], "D0")
                        pg = PS[:, p * 512:(p + 1) * 512]
                        pu = PS[:, 1024 + p * 512:1024 + (p + 1) * 512]
                        b.wait(te, f"W{p}", wgu_done)
                        if gi >= 2:
                            b.wait(te, "SIL", gate_rel.get(p, 0))
                        for dc in range(NDC):
                            xh = FBT["XLH"][:, dc * 512:(dc + 1) * 512]
                            xl = FBT["XLL"][:, dc * 512:(dc + 1) * 512]
                            wh = wgh[:, dc * 128:(dc + 1) * 128]
                            wl = wgl[:, dc * 128:(dc + 1) * 128]
                            te.matmul(pg, wh, xh, start=(dc == 0), stop=False)
                            te.matmul(pg, wh, xl, start=False, stop=False)
                            last = te.matmul(pg, wl, xh, start=False,
                                             stop=(dc == NDC - 1))
                        b.inc(last, "MG")
                        if gi >= 2:
                            b.wait(te, "HH", up_rel.get(p, 0))
                        for dc in range(NDC):
                            xh = FBT["XLH"][:, dc * 512:(dc + 1) * 512]
                            xl = FBT["XLL"][:, dc * 512:(dc + 1) * 512]
                            wh = wuh[:, dc * 128:(dc + 1) * 128]
                            wl = wul[:, dc * 128:(dc + 1) * 128]
                            te.matmul(pu, wh, xh, start=(dc == 0), stop=False)
                            te.matmul(pu, wh, xl, start=False, stop=False)
                            last = te.matmul(pu, wl, xh, start=False,
                                             stop=(dc == NDC - 1))
                        b.inc(last, "MU")
                        wbuf_rel[p] = b.c["MU"]
                        gsb = bigv(L[f"GSB{p}"], 512)
                        b.wait(se, "MG")
                        if gi >= 2:
                            b.wait(se, "HH", up_rel.get(p, 0))
                        ins = se.activation(gsb, pg, AF.Silu)
                        b.inc(ins, "SIL")
                        gate_rel[p] = b.c["SIL"]
                        htmp = bigv(L["HTMP"], 512)
                        hbh = FBT["HBH"][:, il * 512:(il + 1) * 512]
                        hbl = FBT["HBL"][:, il * 512:(il + 1) * 512]
                        b.wait(ve, "SIL")
                        b.wait(ve, "MU")
                        if g >= 1:
                            b.wait(ve, "FL", grp_fl[g - 1])
                        ve.tensor_tensor(htmp, gsb, pu, ALU.mult)
                        ve.tensor_tensor(htmp, htmp, bigv(L["WREPA"] + e * 512, 512),
                                         ALU.mult)
                        ins = ve.tensor_copy(hbh, htmp)
                        b.inc(ins, "VE")
                        b.wait(ve, "VE")
                        ins = ve.tensor_tensor(hbl, htmp, hbh, ALU.subtract)
                        b.inc(ins, "HH")
                        up_rel[p] = b.c["HH"]
                    b.wait(te, "HH")
                    b.wait(te, "D0")
                    for tc in range(4):
                        for dg in range(4):
                            key = dg
                            if key in fl_rel:
                                b.wait(te, "FL", fl_rel[key])
                            pd = PS[:, 2048 + dg * 512:2048 + (dg + 1) * 512]
                            for il in range(IG):
                                hh_ = FBT["HBH"][:, il * 512 + tc * 128:
                                                 il * 512 + tc * 128 + 128]
                                hl_ = FBT["HBL"][:, il * 512 + tc * 128:
                                                 il * 512 + tc * 128 + 128]
                                dh_ = FBT[f"WDH{il}"][:, dg * 512:(dg + 1) * 512]
                                dl_ = FBT[f"WDL{il}"][:, dg * 512:(dg + 1) * 512]
                                te.matmul(pd, hh_, dh_, start=(il == 0), stop=False)
                                te.matmul(pd, hl_, dh_, start=False, stop=False)
                                last = te.matmul(pd, hh_, dl_, start=False,
                                                 stop=(il == IG - 1))
                            b.inc(last, "MD")
                            b.wait(ve, "MD")
                            accs = BIG[:, L["ACC"] + tc * 2048 + dg * 512:
                                       L["ACC"] + tc * 2048 + dg * 512 + 512]
                            if e == 0 and ig == 0:
                                b.wait(ve, "LX")
                                ins = ve.tensor_copy(accs, pd)
                            else:
                                ins = ve.tensor_tensor(accs, accs, pd, ALU.add)
                            b.inc(ins, "FL")
                            b.wait(ve, "FL")
                            fl_rel[key] = b.c["FL"]
                    grp_fl[g] = b.c["FL"]
                    dgrp[0] += 1
            for tc in range(4):
                t0 = q * TQ + tc * 128
                lx = bigv(L["LNX0"], 2048)
                key = 0
                if key in lnx_rel:
                    b.wait(se, "TE", lnx_rel[key])
                b.wait(se, "VE")
                b.dma(se, lx, src[t0:t0 + 128, :], "LX")
                b.wait(ve, "LX")
                b.wait(ve, "FL")
                accs = BIG[:, L["ACC"] + tc * 2048:L["ACC"] + (tc + 1) * 2048]
                ins = ve.scalar_tensor_tensor(accs, accs, 0.5, lx, ALU.mult, ALU.add)
                b.inc(ins, "VE")
                b.wait(ve, "VE")
                b.inc(ve.nop(), "VE")
                b.wait(se, "VE")
                b.dma(se, dst[t0:t0 + 128, :], accs, "LX")
            acc_store[0] = 1

    emit_ffn(0, x_in, x1s)
    b.barrier()
    if stop == "ffn1":
        b.wait(se, "LG")
        b.wait(se, "LX")
        ctx.close()
        return nc

    def build_rope_tables():
        ins = gp.iota(ioti[:, 0:16], [[128, 16]], base=0, channel_multiplier=1)
        b.inc(ins, "GE")
        ins = gp.iota(ioti[0:1, 16:80], [[1, 64]], base=0, channel_multiplier=0)
        b.inc(ins, "GE")
        b.wait(ve, "GE")
        tv = bigv(A["TMPA"], 2048)
        ve.tensor_copy(tv[:, 0:16], ioti[:, 0:16])
        ins = ve.tensor_copy(tv[0:1, 16:80], ioti[0:1, 16:80])
        b.inc(ins, "VE")
        b.wait(se, "VE")
        ins = se.activation(tv[0:1, 80:144], tv[0:1, 16:80], AF.Exp,
                            bias=0.0, scale=-math.log(10000.0) / 64.0)
        b.inc(ins, "SE")
        b.wait(te, "SE")
        pv = PS[:, 0:512]
        ins = te.matmul(pv[:, 0:64], ones1[:], tv[0:1, 80:144], start=True, stop=True)
        b.inc(ins, "TE")
        b.wait(ve, "TE")
        inv_rep = bigv(A["TMPB"], 2048)
        ins = ve.tensor_copy(inv_rep[:, 0:64], pv[:, 0:64])
        b.inc(ins, "VE")
        b.wait(ve, "VE")
        fr = bigv(A["QR"], 2048)
        for tc in range(16):
            ins = ve.tensor_scalar(fr[:, tc * 64:(tc + 1) * 64], inv_rep[:, 0:64],
                                   tv[:, tc:tc + 1], None, ALU.mult)
        b.inc(ins, "VE")
        b.wait(ve, "VE")
        ksv = bigv(A["TOK"], 2048)
        b.dma(sy, ksv[:, 0:1024], ksd[:, :], "LD")
        b.dma(sy, ksv[:, 1024:2048], kcd[:, :], "LD")
        b.wait(ve, "LD")
        sa = bigv(A["TMPA"], 2048)
        ins = ve.scalar_tensor_tensor(sa[:, 0:1024], ksv[:, 0:1024], -TWO_PI_HI,
                                      fr[:, 0:1024], ALU.mult, ALU.add)
        b.inc(ins, "VE")
        b.wait(ve, "VE")
        ins = ve.scalar_tensor_tensor(sa[:, 0:1024], ksv[:, 0:1024], -TWO_PI_LO,
                                      sa[:, 0:1024], ALU.mult, ALU.add)
        b.inc(ins, "VE")
        b.wait(ve, "VE")
        ins = ve.tensor_scalar(sa[:, 0:1024], sa[:, 0:1024], math.pi, -math.pi,
                               ALU.min, ALU.max)
        b.inc(ins, "VE")
        b.wait(se, "VE")
        ins = se.activation(sinT[:], sa[:, 0:1024], AF.Sin)
        b.inc(ins, "SE")
        ins = ve.tensor_scalar(sa[:, 1024:2048], fr[:, 0:1024], math.pi / 2, None, ALU.add)
        b.inc(ins, "VE")
        b.wait(ve, "VE")
        ins = ve.scalar_tensor_tensor(sa[:, 1024:2048], ksv[:, 1024:2048], -TWO_PI_HI,
                                      sa[:, 1024:2048], ALU.mult, ALU.add)
        b.inc(ins, "VE")
        b.wait(ve, "VE")
        ins = ve.scalar_tensor_tensor(sa[:, 1024:2048], ksv[:, 1024:2048], -TWO_PI_LO,
                                      sa[:, 1024:2048], ALU.mult, ALU.add)
        b.inc(ins, "VE")
        b.wait(ve, "VE")
        ins = ve.tensor_scalar(sa[:, 1024:2048], sa[:, 1024:2048], math.pi, -math.pi,
                               ALU.min, ALU.max)
        b.inc(ins, "VE")
        b.wait(se, "VE")
        ins = se.activation(cosT[:], sa[:, 1024:2048], AF.Sin)
        b.inc(ins, "SE")
        b.dma(se, small[0:1, 96:112], qg_in.rearrange("(o a) -> o a", o=1), "LG")
        b.wait(te, "LG")
        b.wait(te, "SE")
        ins = te.matmul(pv[:, 0:16], ones1[:], small[0:1, 96:112],
                        start=True, stop=True)
        b.inc(ins, "TE")
        b.wait(ve, "TE")
        ins = ve.tensor_copy(qgr[:], pv[:, 0:16])
        b.inc(ins, "VE")
        b.wait(ve, "VE")
        with nc.allow_non_contiguous_dma(reason="tiny one-time bias image loads"):
            b.dma(se, biasq[:], bq_in.rearrange("(a p) -> p a", p=128), "LG")
            b.dma(se, biask[:], bk_in.rearrange("(a p) -> p a", p=128), "LG")
            b.dma(se, biasv[:], bv_in.rearrange("(a p) -> p a", p=128), "LG")

    def emit_qkv():
        for q in range(NQ):
            emit_ln_quarter(x1s, q, A["XLNF"], 2048, lx_off=A["QR"])
        if "xlnfd" in dbg:
            b.wait(se, "VE")
            b.dma(se, xlnfd[:, :], BIG[:, 0:32768], "LG")
        chunks = ([("q", ec, qTw[ec], biasq[:, ec:ec + 1], ec) for ec in range(16)]
                  + [("k", ec, kTw[ec], biask[:, ec:ec + 1], 16 + ec)
                     for ec in range(4)]
                  + [("v", ec, vTw[ec], biasv[:, ec:ec + 1], 20 + ec)
                     for ec in range(4)])
        wpj_rel = {}
        for j, (kind, ec, wtile, bias_col, srow) in enumerate(chunks):
            p = j % 2
            if j >= 2:
                b.wait(sy, "TE", wpj_rel[p])
            wpj = bigv(A[f"WPJ{p}"], 2048)
            b.dma(sy, wpj, wtile, f"W{p}")
            b.wait(te, f"W{p}")
            qr = bigv(A["QR"], 2048)
            for tg in range(4):
                pv = PS[:, tg * 512:(tg + 1) * 512]
                b.wait(te, "VE")
                for dc in range(NDC):
                    last = te.matmul(pv, wpj[:, dc * 128:(dc + 1) * 128],
                                     bigv(A["XLNF"] + dc * 2048 + tg * 512, 512),
                                     start=(dc == 0), stop=(dc == NDC - 1))
                b.inc(last, "TE")
                b.wait(ve, "TE")
                ins = ve.tensor_scalar(qr[:, tg * 512:(tg + 1) * 512], pv,
                                       bias_col, None, ALU.add)
                b.inc(ins, "VE")
            wpj_rel[p] = b.c["TE"]
            if kind == "v":
                b.wait(se, "VE")
                b.dma(se, qkvs[srow], qr, "LG")
                continue
            tok = bigv(A["TOK"], 2048)
            for i0 in range(0, 16, 4):
                bank = 4 + (i0 // 4) % 2
                pv = PS[:, bank * 512:(bank + 1) * 512]
                b.wait(te, "VE")
                for qq in range(4):
                    tc = i0 + qq
                    last = te.transpose(pv[:, qq * 128:(qq + 1) * 128],
                                        qr[:, tc * 128:(tc + 1) * 128], ident[:])
                b.inc(last, "TE")
                b.wait(ve, "TE")
                ins = ve.tensor_copy(tok[:, i0 * 128:i0 * 128 + 512], pv)
                b.inc(ins, "VE")
            tmpa = bigv(A["TMPA"], 2048)
            tmpb = bigv(A["TMPB"], 2048)
            b.wait(se, "VE")
            ins = se.activation(tmpa, tok, AF.Square)
            b.inc(ins, "SE")
            b.wait(ve, "SE")
            ins = ve.tensor_reduce(small[:, 32:48], r3(tmpa, 128), AX.X, ALU.add)
            b.inc(ins, "VE")
            b.wait(ve, "VE")
            ins = ve.tensor_scalar(small[:, 32:48], small[:, 32:48],
                                   1.0 / HD, RMSEPS, ALU.mult, ALU.add)
            b.inc(ins, "VE")
            b.wait(se, "VE")
            ins = se.activation(small[:, 48:64], small[:, 32:48], AF.Sqrt)
            b.inc(ins, "SE")
            b.wait(ve, "SE")
            ins = ve.reciprocal(small[:, 64:80], small[:, 48:64])
            b.inc(ins, "VE")
            b.wait(ve, "VE")
            for tc in range(16):
                if kind == "q":
                    ins = ve.tensor_scalar(tok[:, tc * 128:(tc + 1) * 128],
                                           tok[:, tc * 128:(tc + 1) * 128],
                                           small[:, 64 + tc:65 + tc],
                                           qgr[:, ec:ec + 1], ALU.mult, ALU.mult)
                else:
                    ins = ve.tensor_scalar(tok[:, tc * 128:(tc + 1) * 128],
                                           tok[:, tc * 128:(tc + 1) * 128],
                                           small[:, 64 + tc:65 + tc], None, ALU.mult)
            b.inc(ins, "VE")
            b.wait(ve, "VE")
            top = r3(tok, 128)[:, :, 0:64]
            bot = r3(tok, 128)[:, :, 64:128]
            cosv = r3(cosT[:], 64)
            sinv = r3(sinT[:], 64)
            t1 = r3(tmpa, 64)[:, 0:16, :]
            t2 = r3(tmpa, 64)[:, 16:32, :]
            t3 = r3(tmpb, 64)[:, 0:16, :]
            t4 = r3(tmpb, 64)[:, 16:32, :]
            ve.tensor_tensor(t1, top, cosv, ALU.mult)
            ve.tensor_tensor(t2, bot, sinv, ALU.mult)
            ve.tensor_tensor(t3, top, sinv, ALU.mult)
            ve.tensor_tensor(t4, bot, cosv, ALU.mult)
            ve.tensor_tensor(top, t1, t2, ALU.add)
            ins = ve.tensor_tensor(bot, t4, t3, ALU.subtract)
            b.inc(ins, "VE")
            b.wait(te, "VE")
            for i0 in range(0, 16, 4):
                bank = 4 + (i0 // 4) % 2
                pv = PS[:, bank * 512:(bank + 1) * 512]
                b.wait(te, "VE")
                for qq in range(4):
                    tc = i0 + qq
                    last = te.transpose(pv[:, qq * 128:(qq + 1) * 128],
                                        tok[:, tc * 128:(tc + 1) * 128], ident[:])
                b.inc(last, "TE")
                b.wait(ve, "TE")
                ins = ve.tensor_copy(qr[:, i0 * 128:i0 * 128 + 512], pv)
                b.inc(ins, "VE")
            b.wait(se, "VE")
            b.dma(se, qkvs[srow], qr, "LG")

    def emit_masks():
        mk = bigv(C["MASKS"], 2048)
        for tqc in range(4):
            b.wait(gp, "VE")
            ins = gp.iota(ioti[:], [[1, 512]], base=-tqc * 128, channel_multiplier=-1)
            b.inc(ins, "GE")
            b.wait(ve, "GE")
            mv = mk[:, tqc * 512:(tqc + 1) * 512]
            ve.tensor_copy(mv, ioti[:])
            ins = ve.tensor_scalar(mv, mv, 0.0, NEG, ALU.is_gt, ALU.mult)
            b.inc(ins, "VE")
            b.wait(ve, "VE")

    def emit_attn_core():
        mk = bigv(C["MASKS"], 2048)
        sc_rel = [0]
        av_rel = [0]
        b.wait(sy, "LG")
        for g in range(NKV):
            kt = bigv(C["KT"], 2048)
            vt = bigv(C["VT"], 2048)
            vg = bigv(C["VG"], 2048)
            b.wait(sy, "TE")
            b.dma(sy, kt, qkvs[16 + g], "LA")
            b.dma(sy, vt, qkvs[20 + g], "LA")
            b.wait(te, "LA")
            for i0 in range(0, 16, 4):
                pv = PS[:, 3584:4096]
                b.wait(te, "VE")
                for qq in range(4):
                    kc = i0 + qq
                    last = te.transpose(pv[:, qq * 128:(qq + 1) * 128],
                                        vt[:, kc * 128:(kc + 1) * 128], ident[:])
                b.inc(last, "TE")
                b.wait(ve, "TE")
                ins = ve.tensor_copy(vg[:, i0 * 128:i0 * 128 + 512], pv)
                b.inc(ins, "VE")
            for qh in range(4):
                h = g * 4 + qh
                qt = bigv(C["QT"], 2048)
                b.wait(sy, "TE")
                b.dma(sy, qt, qkvs[h], "LB")
                b.wait(te, "LB")
                aot = bigv(C["AOT"], 2048)
                for tqg in range(4):
                    nb = tqg + 1

                    def emit_scores(tqc):
                        tq0 = tqg * 512 + tqc * 128
                        b.wait(te, "VE", sc_rel[0])
                        for kb in range(nb):
                            pv = PS[:, kb * 512:(kb + 1) * 512]
                            ins = te.matmul(pv, qt[:, tq0:tq0 + 128],
                                            kt[:, kb * 512:(kb + 1) * 512],
                                            start=True, stop=True)
                        b.inc(ins, "TE")

                    emit_scores(0)
                    for tqc in range(4):
                        ss = bigv(C["SS"], 2048)
                        pp = bigv(C["PP"], 2048)
                        b.wait(ve, "TE")
                        ins = ve.tensor_copy(ss[:, 0:nb * 512], PS[:, 0:nb * 512])
                        b.inc(ins, "VE")
                        sc_rel[0] = b.c["VE"]
                        dia = ss[:, (nb - 1) * 512: nb * 512]
                        ve.tensor_tensor(dia, dia, mk[:, tqc * 512:(tqc + 1) * 512],
                                         ALU.add)
                        ins = ve.reduce_max(small[:, 0:1], ss[:, 0:nb * 512], AX.X)
                        b.inc(ins, "VE")
                        b.wait(ve, "VE")
                        ins = ve.tensor_scalar(small[:, 1:2], small[:, 0:1], -CSC,
                                               None, ALU.mult)
                        b.inc(ins, "VE")
                        b.wait(se, "VE")
                        ins = se.activation(pp[:, 0:nb * 512], ss[:, 0:nb * 512],
                                            AF.Exp, bias=small[:, 1:2], scale=CSC)
                        b.inc(ins, "SE")
                        b.wait(ve, "SE")
                        ins = ve.reduce_sum(small[:, 2:3], pp[:, 0:nb * 512], AX.X)
                        b.inc(ins, "VE")
                        b.wait(ve, "VE")
                        ins = ve.reciprocal(small[:, 3:4], small[:, 2:3])
                        b.inc(ins, "VE")
                        b.wait(ve, "VE")
                        ins = ve.tensor_scalar(pp[:, 0:nb * 512], pp[:, 0:nb * 512],
                                               small[:, 3:4], None, ALU.mult)
                        b.inc(ins, "VE")
                        if tqc < 3:
                            emit_scores(tqc + 1)
                        for k0 in range(0, nb * 4, 4):
                            bank = 4 + (k0 // 4) % 2
                            pv = PS[:, bank * 512:(bank + 1) * 512]
                            if k0 == 0:
                                b.wait(te, "VE")
                            else:
                                b.wait(te, "VE", b.c["VE"] - 1)
                            for qq in range(4):
                                kc = k0 + qq
                                last = te.transpose(pv[:, qq * 128:(qq + 1) * 128],
                                                    pp[:, kc * 128:(kc + 1) * 128],
                                                    ident[:])
                            b.inc(last, "TE")
                            b.wait(ve, "TE")
                            dstv = r3(BIG[:, C["PT"] + k0 * 512:
                                          C["PT"] + (k0 + 4) * 512], 512)[
                                :, :, tqc * 128: tqc * 128 + 128]
                            ins = ve.tensor_copy(dstv, r3(pv, 128))
                            b.inc(ins, "VE")
                    pv = PS[:, 3072:3584]
                    b.wait(te, "VE")
                    b.wait(te, "VE", av_rel[0])
                    for kc in range(nb * 4):
                        last = te.matmul(pv, vg[:, kc * 128:(kc + 1) * 128],
                                         bigv(C["PT"] + kc * 512, 512),
                                         start=(kc == 0), stop=(kc == nb * 4 - 1))
                    b.inc(last, "TE")
                    b.wait(ve, "TE")
                    ins = ve.tensor_copy(aot[:, tqg * 512:(tqg + 1) * 512], pv)
                    b.inc(ins, "VE")
                    av_rel[0] = b.c["VE"]
                b.wait(se, "VE")
                b.dma(se, aos[h], aot, "LG")

    def emit_oproj():
        borow = bigv(O["XR0"], 2048)
        b.wait(se, "LG")
        b.wait(se, "LX")
        b.dma(se, borow[0:1, :], bo_in.rearrange("(o a) -> o a", o=1), "LX")
        b.wait(te, "LX")
        b.wait(te, "VE")
        for sgl in range(4):
            pv = PS[:, sgl * 512:(sgl + 1) * 512]
            ins = te.matmul(pv, ones1[:], borow[0:1, sgl * 512:(sgl + 1) * 512],
                            start=True, stop=True)
        b.inc(ins, "TE")
        b.wait(ve, "TE")
        ins = ve.tensor_copy(bigv(O["BOREP"], 2048), PS[:, 0:2048])
        b.inc(ins, "VE")
        b.wait(sy, "LG")
        for ec in range(16):
            b.dma(sy, bigv(O["AOTF"] + ec * 2048, 2048), aos[ec], "LA")
        aotf_ld = b.c["LA"]
        wot_rel = {}
        for tcg in range(8):
            for ec in range(16):
                p = ec % 2
                if p in wot_rel:
                    b.wait(sy, "TE", wot_rel[p])
                wot = bigv(O[f"WOT{p}"], 2048)
                b.dma(sy, wot, oTw[ec], f"W{p}")
                b.wait(te, f"W{p}")
                if ec == 0:
                    b.wait(te, "LA", aotf_ld)
                    b.wait(te, "VE")
                for tc2 in range(2):
                    tc = tcg * 2 + tc2
                    for dg in range(4):
                        pv = PS[:, tc2 * 2048 + dg * 512: tc2 * 2048 + (dg + 1) * 512]
                        last = te.matmul(
                            pv,
                            bigv(O["AOTF"] + ec * 2048 + tc * 128, 128),
                            wot[:, dg * 512:(dg + 1) * 512],
                            start=(ec == 0), stop=(ec == 15))
                b.inc(last, "TE")
                wot_rel[p] = b.c["TE"]
            for tc2 in range(2):
                tc = tcg * 2 + tc2
                t0 = tc * 128
                xr = bigv(O[f"XR{tc2}"], 2048)
                b.wait(se, "VE")
                b.dma(se, xr, x1s[t0:t0 + 128, :], "LX")
                b.wait(ve, "TE")
                b.wait(ve, "LX")
                pv = PS[:, tc2 * 2048:(tc2 + 1) * 2048]
                ve.tensor_tensor(xr, xr, pv, ALU.add)
                ins = ve.tensor_tensor(xr, xr, bigv(O["BOREP"], 2048), ALU.add)
                b.inc(ins, "VE")
                b.wait(se, "VE")
                b.dma(se, x2s[t0:t0 + 128, :], xr, "LX")

    build_rope_tables()
    emit_qkv()
    b.barrier()
    emit_masks()
    emit_attn_core()
    b.barrier()
    emit_oproj()
    b.barrier()
    if stop == "attn":
        b.wait(se, "LG")
        b.wait(se, "LX")
        ctx.close()
        return nc

    emit_ffn(1, x2s, y_out)
    b.wait(se, "LG")
    b.wait(se, "LX")
    ctx.close()
    return nc


def prep_inputs(inputs):
    x = np.ascontiguousarray(np.asarray(inputs["L0_attn_x"], dtype=np.float32))
    w = {
        "wq": inputs["L0_attn_Wq"], "bq": inputs["L0_attn_bq"],
        "wk": inputs["L0_attn_Wk"], "bk": inputs["L0_attn_bk"],
        "wv": inputs["L0_attn_Wv"], "bv": inputs["L0_attn_bv"],
        "wo": inputs["L0_attn_Wo"], "bo": inputs["L0_attn_bo"],
        "qg": inputs["L0_attn_q_gain"], "rw": inputs["L0_ffn_router_w"],
        "gw": inputs["L0_ffn_gate_w"], "uw": inputs["L0_ffn_up_w"],
        "dw": inputs["L0_ffn_down_w"],
    }
    w = {k: np.ascontiguousarray(np.asarray(v, dtype=np.float32))
         for k, v in w.items()}
    return x, w


def kernel(**inputs):
    x, w = prep_inputs(inputs)
    nc = build_nc(stop="full")
    in_maps = [dict(x=x[i], **w) for i in range(8)]
    res = run_bass_kernel_spmd(nc, in_maps, core_ids=list(range(8)))
    return np.stack([r["y"] for r in res.results]).astype(np.float32)
